# version 14 (final)
# speedup vs baseline: 1.0942x; 1.0126x over previous
import numpy as np
import ml_dtypes
from contextlib import ExitStack

import concourse.bass as bass
import concourse.mybir as mybir
from concourse.bass_utils import run_bass_kernel_spmd
from concourse.bass_core_rust import IndirectOffsetOnAxis

F32 = mybir.dt.float32
BF16 = mybir.dt.bfloat16
I32 = mybir.dt.int32
AF = mybir.ActivationFunctionType
ALU = mybir.AluOpType

NCORES = 8
C = 2048
T = 8192
TL = T // NCORES
NKC = C // 128
RG = [list(range(NCORES))]

HEAD_DIM0 = 64
NQH = 4
QD = NQH * HEAD_DIM0
INTER_SH = 8192
INTER_R = 1024
INTER_G = 5504
HD1 = 128
NH1 = 2
D1 = NH1 * HD1

EXP_BIAS = -10.0
EXP_SCALE = 0.125
LN_EPS = 1e-5
DEN_EPS = 1e-6

CONSTS = [0.0, 1.0, -1.0, 0.5, -0.5, 2.0, EXP_BIAS, EXP_SCALE, 1.0 / C, LN_EPS,
          DEN_EPS, 0.125]


class Ctx:

    def __init__(self, nc, stack):
        self.nc = nc
        self.stack = stack
        self._sems = {}
        self.cnt = {}
        for name in ["td", "ad", "vd", "gd", "ph"]:
            self.new_sem(name)
        self.phase_idx = 0
        self.cpool = None
        self.aux = {}

    def new_sem(self, name):
        assert name not in self._sems
        self._sems[name] = self.stack.enter_context(self.nc.semaphore(name))
        self.cnt[name] = 0
        return self._sems[name]

    def sem(self, name):
        return self._sems[name]

    def inc(self, instr, name, k=1):
        instr.then_inc(self._sems[name], k)
        self.cnt[name] += k
        return self.cnt[name]

    def val(self, name):
        return self.cnt[name]

    def wait(self, eng, name, v):
        eng.wait_ge(self._sems[name], v)

    def open_phase(self, eng):
        if self.phase_idx > 0:
            eng.wait_ge(self._sems["ph"], 5 * self.phase_idx)

    def close_phase(self, eng, last=None):
        if last is not None:
            last.then_inc(self._sems["ph"], 1)
        else:
            eng.nop().then_inc(self._sems["ph"], 1)

    def end_phase(self):
        self.phase_idx += 1

    def fence(self, e, tag):
        if "fence_in" not in self.aux:
            self.aux["fence_in"] = self.nc.dram_tensor("fence_in", [1, 16], F32).ap()
            self.aux["fence_out"] = self.nc.dram_tensor("fence_out", [NCORES, 16], F32).ap()
            self.new_sem("fence_c")
        cc = e.collective_compute("AllGather", ALU.bypass, replica_groups=RG,
                                  ins=[self.aux["fence_in"]], outs=[self.aux["fence_out"]])
        self.inc(cc, "fence_c")
        e.wait_ge(self.sem("fence_c"), self.val("fence_c"))


def register_consts(ctx):
    nc = ctx.nc
    cpool = ctx.stack.enter_context(nc.sbuf_tensor("cpool", [128, len(CONSTS)], F32))
    for j, cval in enumerate(CONSTS):
        nc.const_aps.register(F32, cval, cpool[:, j:j + 1])
    ctx.cpool = cpool


def ph_init(ctx):
    nc = ctx.nc
    st = ctx.stack
    aux = ctx.aux
    aux["onesc"] = st.enter_context(nc.sbuf_tensor("onesc_f32", [128, 1], F32))
    aux["onesr"] = st.enter_context(nc.sbuf_tensor("onesr_f32", [1, 128], F32))
    with nc.Block() as block:
        @block.gpsimd
        def _(g):
            for j, cval in enumerate(CONSTS):
                g.memset(ctx.cpool[:, j:j + 1], cval)
            g.memset(aux["onesc"][:, :], 1.0)
            g.memset(aux["onesr"][:, :], 1.0)
            ctx.close_phase(g)

        for eng_name in ["sync", "vector", "scalar", "tensor"]:
            getattr(block, eng_name)(lambda e: ctx.close_phase(e))
    ctx.end_phase()


def ph_ln(ctx, tag, src_aps, agin_ap, agout_ap, x1_ap=None):
    nc = ctx.nc
    aux = ctx.aux
    nsrc = len(src_aps)
    with ExitStack() as st:
        x_sb = st.enter_context(nc.sbuf_tensor(f"{tag}_x", [128, NKC * TL], F32))
        sq = [st.enter_context(nc.sbuf_tensor(f"{tag}_sq{i}", [128, TL], F32)) for i in range(2)]
        stat = st.enter_context(nc.sbuf_tensor(f"{tag}_stat", [1, 4 * TL], F32))
        tmp = st.enter_context(nc.sbuf_tensor(f"{tag}_tmp", [128, TL], F32))
        xn_sb = st.enter_context(nc.sbuf_tensor(f"{tag}_xn", [128, NKC * TL], BF16))
        p_sum = st.enter_context(nc.psum_tensor(f"{tag}_psum", [1, TL], F32))
        p_sq = st.enter_context(nc.psum_tensor(f"{tag}_psq", [1, TL], F32))
        p_rstd = st.enter_context(nc.psum_tensor(f"{tag}_prstd", [128, TL], F32))
        p_negb = st.enter_context(nc.psum_tensor(f"{tag}_pnegb", [128, TL], F32))
        sL = ctx.new_sem(f"{tag}_L")
        sP = ctx.new_sem(f"{tag}_P")
        sC = ctx.new_sem(f"{tag}_C")

        td0, ad0, vd0 = ctx.val("td"), ctx.val("ad"), ctx.val("vd")
        vdb = vd0
        ad_sq = [ad0 + f + 1 for f in range(NKC)]
        ad_rstd = ad0 + NKC + 2
        td_sq = [td0 + f + 1 for f in range(NKC)]
        td_bcast = td0 + NKC + 1
        vd_var = vdb + 1
        vd_negb = vdb + 2
        vd_xn = [vdb + 2 + f + 1 for f in range(NKC)]

        with nc.Block() as block:
            @block.sync
            def _(e):
                ctx.open_phase(e)
                if x1_ap is not None:
                    e.wait_ge(sL, 64 * nsrc)
                    x13 = x1_ap.rearrange("(f p) t -> p f t", p=128)
                    xs3 = x_sb[:, :].rearrange("p (f t) -> p f t", f=NKC)
                    for c4 in range(4):
                        ctx.inc(e.dma_start(out=x13[:, 4 * c4:4 * c4 + 4, :],
                                            in_=xs3[:, 4 * c4:4 * c4 + 4, :]), f"{tag}_P", 16)
                ctx.wait(e, "vd", vd_xn[-1])
                ag3 = agin_ap.rearrange("(f p) t -> p f t", p=128)
                xn3 = xn_sb[:, :].rearrange("p (f t) -> p f t", f=NKC)
                for c4 in range(4):
                    ctx.inc(e.dma_start(out=ag3[:, 4 * c4:4 * c4 + 4, :],
                                        in_=xn3[:, 4 * c4:4 * c4 + 4, :]), f"{tag}_P", 16)
                e.wait_ge(sP, 64 + (64 if x1_ap is not None else 0))
                ctx.close_phase(e)

            @block.scalar
            def _(e):
                ctx.open_phase(e)
                e.wait_ge(sL, 64 * nsrc)
                for f in range(NKC):
                    if f >= 2:
                        ctx.wait(e, "td", td_sq[f - 2])
                    ctx.inc(e.activation(sq[f % 2][:, :], x_sb[:, f * TL:(f + 1) * TL], AF.Square), "ad")
                ctx.wait(e, "vd", vd_var)
                ctx.inc(e.activation(stat[0:1, 2 * TL:3 * TL], stat[0:1, TL:2 * TL], AF.Ln, bias=LN_EPS), "ad")
                ctx.inc(e.activation(stat[0:1, 2 * TL:3 * TL], stat[0:1, 2 * TL:3 * TL], AF.Exp, scale=-0.5), "ad")
                assert ctx.val("ad") == ad_rstd
                ctx.close_phase(e)

            @block.tensor
            def _(e):
                ctx.open_phase(e)
                e.wait_ge(sL, 64 * nsrc)
                for f in range(NKC):
                    for h in range(2):
                        e.matmul(p_sum[0:1, h * 512:(h + 1) * 512], aux["onesc"][:, :],
                                 x_sb[:, f * TL + h * 512: f * TL + (h + 1) * 512],
                                 start=(f == 0), stop=(f == NKC - 1))
                    ctx.wait(e, "ad", ad_sq[f])
                    for h in range(2):
                        mm = e.matmul(p_sq[0:1, h * 512:(h + 1) * 512], aux["onesc"][:, :],
                                      sq[f % 2][:, h * 512:(h + 1) * 512],
                                      start=(f == 0), stop=(f == NKC - 1))
                    ctx.inc(mm, "td")
                ctx.wait(e, "vd", vd_negb)
                ctx.wait(e, "ad", ad_rstd)
                for h in range(2):
                    e.matmul(p_rstd[:, h * 512:(h + 1) * 512], aux["onesr"][:, :],
                             stat[0:1, 2 * TL + h * 512:2 * TL + (h + 1) * 512], start=True, stop=True)
                    mm = e.matmul(p_negb[:, h * 512:(h + 1) * 512], aux["onesr"][:, :],
                                  stat[0:1, 3 * TL + h * 512:3 * TL + (h + 1) * 512], start=True, stop=True)
                ctx.inc(mm, "td")
                assert ctx.val("td") == td_bcast
                ctx.close_phase(e)

            @block.vector
            def _(e):
                ctx.open_phase(e)
                ctx.wait(e, "td", td_sq[-1])
                e.tensor_scalar(stat[0:1, 0:TL], p_sum[0:1, :], 1.0 / C, None, ALU.mult)
                e.tensor_scalar(stat[0:1, TL:2 * TL], p_sq[0:1, :], 1.0 / C, None, ALU.mult)
                e.tensor_tensor(tmp[0:1, 0:TL], stat[0:1, 0:TL], stat[0:1, 0:TL], ALU.mult)
                ctx.inc(e.tensor_tensor(stat[0:1, TL:2 * TL], stat[0:1, TL:2 * TL], tmp[0:1, 0:TL], ALU.subtract), "vd")
                ctx.wait(e, "ad", ad_rstd)
                ctx.inc(e.tensor_tensor(stat[0:1, 3 * TL:4 * TL], stat[0:1, 0:TL], stat[0:1, 2 * TL:3 * TL], ALU.mult), "vd")
                ctx.wait(e, "td", td_bcast)
                for f in range(NKC):
                    e.tensor_tensor(tmp[:, :], x_sb[:, f * TL:(f + 1) * TL], p_rstd[:, :], ALU.mult)
                    ctx.inc(e.tensor_tensor(xn_sb[:, f * TL:(f + 1) * TL], tmp[:, :], p_negb[:, :], ALU.subtract), "vd")
                assert ctx.val("vd") == vd_xn[-1]
                ctx.close_phase(e)

            @block.gpsimd
            def _(e):
                ctx.open_phase(e)
                for si, sap in enumerate(src_aps):
                    src3 = sap.rearrange("(f p) t -> p f t", p=128)
                    dst3 = x_sb[:, :].rearrange("p (f t) -> p f t", f=NKC)
                    if si > 0:
                        e.wait_ge(sL, 64 * si)
                    for c4 in range(4):
                        ctx.inc(e.dma_start(out=dst3[:, 4 * c4:4 * c4 + 4, :],
                                            in_=src3[:, 4 * c4:4 * c4 + 4, :],
                                            accum_op=(ALU.add if si > 0 else ALU.bypass)),
                                f"{tag}_L", 16)
                e.wait_ge(sP, 64 + (64 if x1_ap is not None else 0))
                ctx.fence(e, tag)
                cc = e.collective_compute("AllGather", ALU.bypass, replica_groups=RG,
                                          ins=[agin_ap], outs=[agout_ap])
                ctx.inc(cc, f"{tag}_C")
                e.wait_ge(sC, ctx.val(f"{tag}_C"))
                ctx.close_phase(e)
    ctx.end_phase()


def ph_qkv0(ctx, tag, xnall_ap, w0_ap, cos4_ap, sinF4_ap, q_ap, k_ap, v_ap,
            idx_casts=None):
    nc = ctx.nc
    NSUB = 8
    with ExitStack() as st:
        w0_sb = st.enter_context(nc.sbuf_tensor(f"{tag}_w0", [128, NKC * 384], BF16))
        xn_b = [st.enter_context(nc.sbuf_tensor(f"{tag}_xn{i}", [128, NKC * TL], BF16)) for i in range(2)]
        cos_b = [st.enter_context(nc.sbuf_tensor(f"{tag}_cos{i}", [128, 256], F32)) for i in range(2)]
        sin_b = [st.enter_context(nc.sbuf_tensor(f"{tag}_sin{i}", [128, 256], F32)) for i in range(2)]
        qo_b = [st.enter_context(nc.sbuf_tensor(f"{tag}_qo{i}", [128, NSUB * QD], BF16)) for i in range(2)]
        ko_b = [st.enter_context(nc.sbuf_tensor(f"{tag}_ko{i}", [128, NSUB * 128], BF16)) for i in range(2)]
        vo_b = [st.enter_context(nc.sbuf_tensor(f"{tag}_vo{i}", [128, NSUB * 65], BF16)) for i in range(2)]
        trot = st.enter_context(nc.sbuf_tensor(f"{tag}_trot", [128, 256], F32))
        tcos = st.enter_context(nc.sbuf_tensor(f"{tag}_tcos", [128, 256], F32))
        ps = [st.enter_context(nc.psum_tensor(f"{tag}_ps{i}", [128, 384], F32)) for i in range(2)]
        sW = ctx.new_sem(f"{tag}_W")
        sIC = ctx.new_sem(f"{tag}_IC")
        sX = [ctx.new_sem(f"{tag}_X{i}") for i in range(2)]
        sCs = [ctx.new_sem(f"{tag}_Cs{i}") for i in range(2)]
        sO = [ctx.new_sem(f"{tag}_O{i}") for i in range(2)]

        td0, vd0, gd0 = ctx.val("td"), ctx.val("vd"), ctx.val("gd")
        td_sub = [td0 + S + 1 for S in range(T // 128)]
        vd_sub = [vd0 + S + 1 for S in range(T // 128)]
        NT = T // 128
        gd_vo = gd0 + 2

        with nc.Block() as block:
            @block.gpsimd
            def _(e):
                ctx.open_phase(e)
                if idx_casts:
                    for dst_ap, src_ap_ in idx_casts:
                        ctx.inc(e.dma_start(out=dst_ap, in_=src_ap_), f"{tag}_IC", 16)
                for i in range(2):
                    e.memset(vo_b[i][:, :], 1.0)
                    e.memset(ko_b[i][:, :], 0.0)
                mv = e.nop()
                ctx.inc(mv, "gd", 2)
                ctx.inc(e.dma_start(out=w0_sb[:, :].rearrange("p (f n) -> p f n", f=NKC),
                                    in_=w0_ap.rearrange("(f p) n -> p f n", p=128)), f"{tag}_W", 16)
                for tb in range(NCORES):
                    if tb >= 2:
                        ctx.wait(e, "td", td_sub[(tb - 2) * NSUB + NSUB - 1])
                    src3 = xnall_ap.rearrange("(b f p) t -> b p f t", b=NCORES, p=128)
                    dst3 = xn_b[tb % 2][:, :].rearrange("p (f t) -> p f t", f=NKC)
                    for c4 in range(4):
                        ctx.inc(e.dma_start(out=dst3[:, 4 * c4:4 * c4 + 4, :],
                                            in_=src3[tb, :, 4 * c4:4 * c4 + 4, :]), f"{tag}_X{tb % 2}", 16)
                if idx_casts:
                    e.wait_ge(sIC, 16 * len(idx_casts))
                ctx.close_phase(e)

            @block.sync
            def _(e):
                ctx.open_phase(e)
                for S in range(NT):
                    if S >= 2:
                        ctx.wait(e, "vd", vd_sub[S - 2])
                    ctx.inc(e.dma_start(out=cos_b[S % 2][:, :], in_=cos4_ap[S * 128:(S + 1) * 128, :]), f"{tag}_Cs{S % 2}", 16)
                    ctx.inc(e.dma_start(out=sin_b[S % 2][:, :], in_=sinF4_ap[S * 128:(S + 1) * 128, :]), f"{tag}_Cs{S % 2}", 16)
                    if S % NSUB == NSUB - 1:
                        tb = S // NSUB
                        ctx.wait(e, "vd", vd_sub[S])
                        pb = tb % 2
                        ctx.inc(e.dma_start(out=q_ap.rearrange("(b s p) c -> b p s c", b=NCORES, p=128)[tb],
                                            in_=qo_b[pb][:, :].rearrange("p (s c) -> p s c", s=NSUB)), f"{tag}_O{pb}", 16)
                        ctx.inc(e.dma_start(out=k_ap.rearrange("(b s p) c -> b p s c", b=NCORES, p=128)[tb],
                                            in_=ko_b[pb][:, :].rearrange("p (s c) -> p s c", s=NSUB)), f"{tag}_O{pb}", 16)
                        ctx.inc(e.dma_start(out=v_ap.rearrange("(b s p) c -> b p s c", b=NCORES, p=128)[tb],
                                            in_=vo_b[pb][:, :].rearrange("p (s c) -> p s c", s=NSUB)), f"{tag}_O{pb}", 16)
                e.wait_ge(sO[0], 16 * 3 * 4)
                e.wait_ge(sO[1], 16 * 3 * 4)
                ctx.close_phase(e)

            @block.tensor
            def _(e):
                ctx.open_phase(e)
                e.wait_ge(sW, 16)
                for S in range(NT):
                    tb, s = divmod(S, NSUB)
                    if s == 0:
                        e.wait_ge(sX[tb % 2], 64 * (tb // 2 + 1))
                    if S >= 2:
                        ctx.wait(e, "vd", vd_sub[S - 2])
                    for kc in range(NKC):
                        mm = e.matmul(ps[S % 2][:, :], xn_b[tb % 2][:, kc * TL + s * 128: kc * TL + (s + 1) * 128],
                                      w0_sb[:, kc * 384:(kc + 1) * 384],
                                      start=(kc == 0), stop=(kc == NKC - 1))
                    ctx.inc(mm, "td")
                ctx.close_phase(e)

            @block.vector
            def _(e):
                ctx.open_phase(e)
                ctx.wait(e, "gd", gd_vo)
                for S in range(NT):
                    tb, s = divmod(S, NSUB)
                    pb = tb % 2
                    ctx.wait(e, "td", td_sub[S])
                    e.wait_ge(sCs[S % 2], 32 * (S // 2 + 1))
                    if s == 0 and tb >= 2:
                        e.wait_ge(sO[pb], 48 * (tb // 2))
                    p = ps[S % 2]
                    q3 = p[:, 0:QD].rearrange("p (h d) -> p h d", h=NQH)
                    c3 = cos_b[S % 2][:, :].rearrange("p (h d) -> p h d", h=NQH)
                    s3 = sin_b[S % 2][:, :].rearrange("p (h d) -> p h d", h=NQH)
                    t3 = trot[:, :].rearrange("p (h d) -> p h d", h=NQH)
                    e.tensor_tensor(t3[:, :, 0:32], q3[:, :, 32:64], s3[:, :, 0:32], ALU.mult)
                    e.tensor_tensor(t3[:, :, 32:64], q3[:, :, 0:32], s3[:, :, 32:64], ALU.mult)
                    e.tensor_tensor(tcos[:, :], p[:, 0:QD], cos_b[S % 2][:, :], ALU.mult)
                    e.tensor_tensor(qo_b[pb][:, s * QD:(s + 1) * QD], tcos[:, :], trot[:, :], ALU.add)
                    kk = p[:, QD:QD + 64]
                    e.tensor_tensor(trot[:, 0:32], kk[:, 32:64], sin_b[S % 2][:, 0:32], ALU.mult)
                    e.tensor_tensor(trot[:, 32:64], kk[:, 0:32], sin_b[S % 2][:, 32:64], ALU.mult)
                    e.tensor_tensor(tcos[:, 0:64], kk[:, :], cos_b[S % 2][:, 0:64], ALU.mult)
                    e.tensor_tensor(ko_b[pb][:, s * 128:s * 128 + 64], tcos[:, 0:64], trot[:, 0:64], ALU.add)
                    e.tensor_tensor(ko_b[pb][:, s * 128 + 64:s * 128 + 128], tcos[:, 0:64], trot[:, 0:64], ALU.add)
                    ctx.inc(e.tensor_copy(vo_b[pb][:, s * 65:s * 65 + 64], p[:, QD + 64:QD + 128]), "vd")
                assert ctx.val("vd") == vd_sub[-1]
                ctx.close_phase(e)

            @block.scalar
            def _(e):
                ctx.open_phase(e)
                ctx.close_phase(e)
    ctx.end_phase()


def _rope_tables():
    pos = np.arange(T, dtype=np.float64)
    dim = np.arange(0, HEAD_DIM0, 2, dtype=np.float64)
    freqs = 10000.0 ** (-dim / HEAD_DIM0)
    theta = np.outer(pos, freqs)
    cos = np.concatenate([np.cos(theta), np.cos(theta)], axis=1)
    sin = np.concatenate([np.sin(theta), np.sin(theta)], axis=1)
    sinF = sin.copy()
    sinF[:, 0:32] *= -1.0
    cos4 = np.tile(cos, (1, NQH)).astype(np.float32)
    sinF4 = np.tile(sinF, (1, NQH)).astype(np.float32)
    return np.ascontiguousarray(cos4), np.ascontiguousarray(sinF4)


def _mask4():
    m = np.zeros((4 * 128, 512), dtype=np.float32)
    for rel in range(4):
        for ki in range(128):
            qi = np.arange(512)
            m[rel * 128 + ki] = (qi >= 128 * rel + ki).astype(np.float32)
    return m.astype(ml_dtypes.bfloat16)


def prep_inputs(inp):
    x = np.asarray(inp["L0_attn_x"])[0]
    xT = np.ascontiguousarray(x.T)
    wq, wk, wv = (np.asarray(inp[k]) for k in ("L0_attn_wq", "L0_attn_wk", "L0_attn_wv"))
    wqkv = np.asarray(inp["L1_attn_w_qkv"])
    cos4, sinF4 = _rope_tables()
    shared = {
        "cos4": cos4, "sinF4": sinF4, "mask4": _mask4(),
        "wo": np.ascontiguousarray(np.asarray(inp["L0_attn_wo"])),
        "shgT": np.ascontiguousarray(np.asarray(inp["L0_ffn_sh_gate"]).T),
        "shuT": np.ascontiguousarray(np.asarray(inp["L0_ffn_sh_up"]).T),
        "shdT": np.ascontiguousarray(np.asarray(inp["L0_ffn_sh_down"]).T),
        "rgT": np.ascontiguousarray(np.asarray(inp["L0_ffn_r_gate"]).T),
        "ruT": np.ascontiguousarray(np.asarray(inp["L0_ffn_r_up"]).T),
        "rdT": np.ascontiguousarray(np.asarray(inp["L0_ffn_r_down"]).T),
        "w_out": np.ascontiguousarray(np.asarray(inp["L1_attn_w_out"])),
        "lut": np.ascontiguousarray(np.asarray(inp["L1_ffn_lut"]).reshape(1, 16)),
    }
    gateT = np.ascontiguousarray(np.asarray(inp["L1_ffn_gate_idx"]).T)
    upT = np.ascontiguousarray(np.asarray(inp["L1_ffn_up_idx"]).T)
    downT = np.ascontiguousarray(np.asarray(inp["L1_ffn_down_idx"]).T)
    dec = np.asarray(inp["L1_attn_decay_param"])
    percore = []
    for j in range(NCORES):
        pc = {
            "xt": np.ascontiguousarray(xT[:, TL * j:TL * (j + 1)]),
            "w0": np.ascontiguousarray(np.concatenate(
                [wq[:, QD * j:QD * (j + 1)], wk[:, 64 * j:64 * (j + 1)],
                 wv[:, 64 * j:64 * (j + 1)]], axis=1)),
            "w1": np.ascontiguousarray(np.concatenate(
                [wqkv[:, D1 * j:D1 * (j + 1)],
                 wqkv[:, C + D1 * j:C + D1 * (j + 1)],
                 wqkv[:, 2 * C + D1 * j:2 * C + D1 * (j + 1)]], axis=1)),
            "dec": np.ascontiguousarray(dec[NH1 * j:NH1 * (j + 1)].reshape(1, NH1)),
            "gidx": (8 * (128 * np.arange(NKC, dtype=np.int32)[None, :]
                          + np.arange(128, dtype=np.int32)[:, None]) + j),
            "gidxT_g": np.ascontiguousarray(gateT[256 * j:256 * (j + 1), :]),
            "gidxT_u": np.ascontiguousarray(upT[256 * j:256 * (j + 1), :]),
            "gidxT_d": np.ascontiguousarray(downT[688 * j:688 * (j + 1), :]),
        }
        percore.append(pc)
    return shared, percore


TENSORS = {
    "xt": ([C, TL], F32), "w0": ([C, 384], F32),
    "cos4": ([T, 256], F32), "sinF4": ([T, 256], F32), "mask4": ([512, 512], BF16),
    "wo": ([C, C], F32), "w_out": ([C, C], F32),
    "shgT": ([C, INTER_SH], F32), "shuT": ([C, INTER_SH], F32), "shdT": ([INTER_SH, C], F32),
    "rgT": ([C, INTER_R], F32), "ruT": ([C, INTER_R], F32), "rdT": ([INTER_R, C], F32),
    "w1": ([C, 768], F32), "dec": ([1, NH1], F32), "lut": ([1, 16], F32),
    "gidx": ([128, NKC], I32),
    "gidxT_g": ([256, INTER_G], I32), "gidxT_u": ([256, INTER_G], I32),
    "gidxT_d": ([688, C], I32),
    "xn0_loc": ([C, TL], BF16), "xn0_ag": ([NCORES * C, TL], BF16),
    "q0": ([T, QD], BF16), "k0": ([T, 128], BF16), "v0": ([T, 65], BF16),
    "o0_fm": ([QD, T], BF16), "o0_agin": ([QD, T], BF16), "o0_ag": ([NCORES * QD, T], BF16),
    "attn0_d": ([C, TL], F32), "ffn0_d": ([C, TL], F32),
    "x1": ([C, TL], F32), "xn1_loc": ([C, TL], BF16), "xn1_ag": ([NCORES * C, TL], BF16),
    "o1_fm": ([D1, T], BF16), "o1_agin": ([D1, T], BF16), "o1_ag": ([NCORES * D1, T], BF16),
    "gw_g": ([C, INTER_G], BF16), "gw_u": ([C, INTER_G], BF16), "gw_d": ([INTER_G, C], BF16),
    "gw_g_in": ([256, INTER_G], BF16), "gw_u_in": ([256, INTER_G], BF16), "gw_d_in": ([688, C], BF16),
    "gib_g": ([256, INTER_G], BF16), "gib_u": ([256, INTER_G], BF16), "gib_d": ([688, C], BF16),
    "attn1_d": ([C, TL], F32), "ffn1_d": ([C, TL], F32),
    "y": ([C, TL], F32),
}


def ph_attn0(ctx, tag, q_ap, k_ap, v_ap, mask_ap, agin_ap, agout_ap):
    nc = ctx.nc
    aux = ctx.aux
    NQC = 16
    pairs = [(qc, h) for qc in range(NQC) for h in range(NQH)]

    with ExitStack() as st:
        qf = [st.enter_context(nc.sbuf_tensor(f"{tag}_qf{i}", [128, T], BF16)) for i in range(2)]
        kf = st.enter_context(nc.sbuf_tensor(f"{tag}_kf", [128, T], BF16))
        v_sb = st.enter_context(nc.sbuf_tensor(f"{tag}_v", [128, 64 * 65], BF16))
        m_sb = st.enter_context(nc.sbuf_tensor(f"{tag}_m", [128, 4 * 512], BF16))
        p_sb = [st.enter_context(nc.sbuf_tensor(f"{tag}_p{i}", [128, 1024], BF16)) for i in range(2)]
        otmp = [st.enter_context(nc.sbuf_tensor(f"{tag}_ot{i}", [65, 512], F32)) for i in range(2)]
        rrow = [st.enter_context(nc.sbuf_tensor(f"{tag}_rr{i}", [1, 512], F32)) for i in range(2)]
        finb = [st.enter_context(nc.sbuf_tensor(f"{tag}_fin{i}", [64, 512], BF16)) for i in range(2)]
        S_ps = [st.enter_context(nc.psum_tensor(f"{tag}_S{i}", [128, 1024], F32)) for i in range(2)]
        O_ps = [st.enter_context(nc.psum_tensor(f"{tag}_O{i}", [65, 512], F32)) for i in range(2)]
        lb_ps = [st.enter_context(nc.psum_tensor(f"{tag}_lb{i}", [64, 512], F32)) for i in range(2)]
        sLA = ctx.new_sem(f"{tag}_L")
        sP = ctx.new_sem(f"{tag}_P")
        sPf = [ctx.new_sem(f"{tag}_Pf{i}") for i in range(2)]
        sC = ctx.new_sem(f"{tag}_C")

        td0, ad0, vd0 = ctx.val("td"), ctx.val("ad"), ctx.val("vd")
        td_S, td_PV, td_bc = {}, {}, {}
        ad_exp, ad_evac, ad_recip = {}, {}, {}
        vd_mask, vd_fin = {}, {}
        gS_base = {}
        tdv, adv, vdv = td0, ad0, vd0
        gS = 0
        td_S_flat, td_PV_flat, ad_exp_flat = [], [], []
        for idx, (qc, h) in enumerate(pairs):
            B = 2 * (qc + 1)
            gS_base[idx] = gS
            gS += B
            td_S[idx] = [tdv + i + 1 for i in range(B)]
            tdv += B
            td_S_flat += td_S[idx]
        tdv = td0
        adv = ad0
        vdv = vd0
        gS = 0
        td_S_flat, td_PV_flat, ad_exp_flat = [], [], []
        pe_order = {}
        for idx, (qc, h) in enumerate(pairs):
            B = 2 * (qc + 1)
            gS_base[idx] = gS
            order = [("S", 0), ("S", 1)]
            for b in range(B - 2):
                order.append(("PV", b))
                order.append(("S", b + 2))
            order.append(("PV", B - 2))
            order.append(("PV", B - 1))
            pe_order[idx] = order
            td_S[idx] = [None] * B
            td_PV[idx] = [None] * B
            for kind, b in order:
                tdv += 1
                if kind == "S":
                    td_S[idx][b] = tdv
                else:
                    td_PV[idx][b] = tdv
            td_bc[idx] = tdv + 1
            tdv += 1
            td_S_flat += td_S[idx]
            td_PV_flat += td_PV[idx]
            ad_exp[idx] = [adv + i + 1 for i in range(B)]
            adv += B
            ad_exp_flat += ad_exp[idx]
            ad_evac[idx] = adv + 1
            ad_recip[idx] = adv + 3
            adv += 3
            vd_mask[idx] = [vdv + i + 1 for i in range(4)]
            vdv += 4
            vd_fin[idx] = vdv + 1
            vdv += 1
            gS += B
        td_end, ad_end, vd_end = tdv, adv, vdv

        with nc.Block() as block:
            @block.sync
            def _(e):
                ctx.open_phase(e)
                for i in range(2):
                    ctx.inc(e.dma_start(out=qf[i][:, :], in_=q_ap[:, 128 * i:128 * (i + 1)],
                                        transpose=True), f"{tag}_L", 16)
                ctx.inc(e.dma_start(out=kf[:, :], in_=k_ap, transpose=True), f"{tag}_L", 16)
                ctx.inc(e.dma_start(out=v_sb[:, :].rearrange("p (s c) -> p s c", s=64),
                                    in_=v_ap.rearrange("(s p) c -> p s c", p=128)), f"{tag}_L", 16)
                ctx.inc(e.dma_start(out=m_sb[:, :].rearrange("p (r q) -> p r q", r=4),
                                    in_=mask_ap.rearrange("(r p) q -> p r q", p=128)), f"{tag}_L", 16)
                ctx.wait(e, "vd", vd_end)
                for h in range(NQH):
                    ctx.inc(e.dma_start(out=agin_ap[64 * h:64 * (h + 1), :],
                                        in_=ofm4[h][:, :]), f"{tag}_P", 16)
                e.wait_ge(sP, 64)
                ctx.close_phase(e)

            @block.tensor
            def _(e):
                ctx.open_phase(e)
                e.wait_ge(sLA, 80)
                for idx, (qc, h) in enumerate(pairs):
                    B = 2 * (qc + 1)
                    qslice = qf[h // 2][64 * (h % 2):64 * (h % 2) + 64, 512 * qc:512 * (qc + 1)]
                    for kind, b in pe_order[idx]:
                        g = gS_base[idx] + b
                        if kind == "S":
                            if g >= 2:
                                ctx.wait(e, "ad", ad_exp_flat[g - 2])
                            for i in range(2):
                                kt = 2 * b + i
                                mm = e.matmul(S_ps[g % 2][:, 512 * i:512 * (i + 1)],
                                              kf[64 * (h % 2):64 * (h % 2) + 64, 128 * kt:128 * (kt + 1)], qslice,
                                              start=True, stop=True)
                            ctx.inc(mm, "td")
                        else:
                            ctx.wait(e, "ad", ad_exp[idx][b])
                            for i in range(2):
                                kt = 2 * b + i
                                rel = kt - 4 * qc
                                if rel >= 0:
                                    ctx.wait(e, "vd", vd_mask[idx][rel])
                                if kt == 0 and idx >= 2:
                                    ctx.wait(e, "vd", vd_fin[idx - 2])
                                mm = e.matmul(O_ps[idx % 2][:, :],
                                              v_sb[:, 65 * kt:65 * (kt + 1)],
                                              p_sb[g % 2][:, 512 * i:512 * (i + 1)],
                                              start=(kt == 0), stop=(kt == 4 * (qc + 1) - 1))
                            ctx.inc(mm, "td")
                    ctx.wait(e, "ad", ad_recip[idx])
                    if idx >= 2:
                        ctx.wait(e, "vd", vd_fin[idx - 2])
                    mm = e.matmul(lb_ps[idx % 2][:, :], aux["onesr"][0:1, 0:64],
                                  rrow[idx % 2][0:1, :], start=True, stop=True)
                    ctx.inc(mm, "td")
                assert ctx.val("td") == td_end
                ctx.close_phase(e)

            @block.scalar
            def _(e):
                ctx.open_phase(e)
                for idx, (qc, h) in enumerate(pairs):
                    B = 2 * (qc + 1)
                    for b in range(B):
                        g = gS_base[idx] + b
                        ctx.wait(e, "td", td_S[idx][b])
                        if g >= 2:
                            ctx.wait(e, "td", td_PV_flat[g - 2])
                        ctx.inc(e.activation(p_sb[g % 2][:, :], S_ps[g % 2][:, :], AF.Exp,
                                             bias=EXP_BIAS, scale=EXP_SCALE), "ad")
                    ctx.wait(e, "td", td_PV[idx][B - 1])
                    if idx >= 2:
                        ctx.wait(e, "vd", vd_fin[idx - 2])
                    ctx.inc(e.copy(otmp[idx % 2][:, :], O_ps[idx % 2][:, :]), "ad")
                    e.activation(rrow[idx % 2][0:1, :], otmp[idx % 2][64:65, :], AF.Ln)
                    ctx.inc(e.activation(rrow[idx % 2][0:1, :], rrow[idx % 2][0:1, :], AF.Exp, scale=-1.0), "ad", 2)
                assert ctx.val("ad") == ad_end
                ctx.close_phase(e)

            @block.vector
            def _(e):
                ctx.open_phase(e)
                for idx, (qc, h) in enumerate(pairs):
                    for rel in range(4):
                        kt = 4 * qc + rel
                        b = kt // 2
                        g = gS_base[idx] + b
                        i = kt % 2
                        ctx.wait(e, "ad", ad_exp[idx][b])
                        ctx.inc(e.tensor_tensor(p_sb[g % 2][:, 512 * i:512 * (i + 1)],
                                                p_sb[g % 2][:, 512 * i:512 * (i + 1)],
                                                m_sb[:, 512 * rel:512 * (rel + 1)], ALU.mult), "vd")
                    ctx.wait(e, "td", td_bc[idx])
                    ctx.wait(e, "ad", ad_evac[idx])
                    dst = ofm4[h][:, 512 * qc:512 * (qc + 1)]
                    ctx.inc(e.tensor_tensor(dst, otmp[idx % 2][0:64, :], lb_ps[idx % 2][:, :], ALU.mult), "vd")
                assert ctx.val("vd") == vd_end
                ctx.close_phase(e)

            @block.gpsimd
            def _(e):
                ctx.open_phase(e)
                e.wait_ge(sP, 16)
                ctx.fence(e, tag)
                cc = e.collective_compute("AllGather", ALU.bypass, replica_groups=RG,
                                          ins=[agin_ap], outs=[agout_ap])
                ctx.inc(cc, f"{tag}_C")
                e.wait_ge(sC, ctx.val(f"{tag}_C"))
                ctx.close_phase(e)
    ctx.end_phase()


def ph_gather_womm(ctx, tag, agout_ap, gidx_ap, w_ap, out_ap, extra_src_aps=None):
    nc = ctx.nc
    nex = len(extra_src_aps) if extra_src_aps else 0
    with ExitStack() as st:
        oloc = st.enter_context(nc.sbuf_tensor(f"{tag}_ol", [128, NKC * TL], BF16))
        gidx_sb = st.enter_context(nc.sbuf_tensor(f"{tag}_gi", [128, NKC], I32))
        wsb = [st.enter_context(nc.sbuf_tensor(f"{tag}_w{i}", [128, NKC * 512], BF16)) for i in range(2)]
        out_sb = st.enter_context(nc.sbuf_tensor(f"{tag}_os", [128, NKC * TL], F32))
        ps = [st.enter_context(nc.psum_tensor(f"{tag}_ps{i}", [128, 512], F32)) for i in range(8)]
        sG = ctx.new_sem(f"{tag}_G")
        sW = [ctx.new_sem(f"{tag}_W{i}") for i in range(2)]
        sS = ctx.new_sem(f"{tag}_S")
        sE = ctx.new_sem(f"{tag}_E")

        td0, vd0 = ctx.val("td"), ctx.val("vd")
        td_grp = [td0 + g + 1 for g in range(4)]
        vd_grp = [vd0 + 8 * (g + 1) for g in range(4)]
        agflat = agout_ap.rearrange("d (b t) -> (d b) t", b=NCORES)

        with nc.Block() as block:
            @block.gpsimd
            def _(e):
                ctx.open_phase(e)
                o4 = out_sb[:, :].rearrange("p (f t) -> p f t", f=NKC)
                for xi in range(nex):
                    if xi >= 1:
                        e.wait_ge(sE, 64 * xi)
                    x3 = extra_src_aps[xi].rearrange("(f p) t -> p f t", p=128)
                    for c4 in range(4):
                        ctx.inc(e.dma_start(out=o4[:, 4 * c4:4 * c4 + 4, :],
                                            in_=x3[:, 4 * c4:4 * c4 + 4, :],
                                            accum_op=(ALU.bypass if xi == 0 else ALU.add)),
                                f"{tag}_E", 16)
                ctx.inc(e.dma_start(out=gidx_sb[:, :], in_=gidx_ap), f"{tag}_G", 16)
                e.wait_ge(sG, 16)
                for f in range(NKC):
                    ctx.inc(e.indirect_dma_start(
                        oloc[:, f * TL:(f + 1) * TL], None,
                        agflat, IndirectOffsetOnAxis(gidx_sb[:, f:f + 1], 0)), f"{tag}_G", 16)
                for g in range(4):
                    if g >= 2:
                        ctx.wait(e, "td", td_grp[g - 2])
                    w3 = w_ap.rearrange("(kc p) c -> p kc c", p=128)
                    ctx.inc(e.dma_start(out=wsb[g % 2][:, :].rearrange("p (kc c) -> p kc c", kc=NKC),
                                        in_=w3[:, :, 512 * g:512 * (g + 1)]), f"{tag}_W{g % 2}", 16)
                ctx.close_phase(e)

            @block.tensor
            def _(e):
                ctx.open_phase(e)
                e.wait_ge(sG, 16 * 17)
                for g in range(4):
                    e.wait_ge(sW[g % 2], 16 * (g // 2 + 1))
                    if g >= 1:
                        ctx.wait(e, "vd", vd_grp[g - 1])
                    for kc in range(NKC):
                        for fs in range(4):
                            for th in range(2):
                                mm = e.matmul(ps[fs * 2 + th][:, :],
                                              wsb[g % 2][:, kc * 512 + 128 * fs:kc * 512 + 128 * (fs + 1)],
                                              oloc[:, kc * TL + 512 * th:kc * TL + 512 * (th + 1)],
                                              start=(kc == 0), stop=(kc == NKC - 1))
                    ctx.inc(mm, "td")
                ctx.close_phase(e)

            @block.vector
            def _(e):
                ctx.open_phase(e)
                if nex:
                    e.wait_ge(sE, 64 * nex)
                for g in range(4):
                    ctx.wait(e, "td", td_grp[g])
                    for fs in range(4):
                        fo = 4 * g + fs
                        for th in range(2):
                            sl = slice(fo * TL + 512 * th, fo * TL + 512 * (th + 1))
                            if nex:
                                ctx.inc(e.tensor_tensor(out_sb[:, sl], out_sb[:, sl],
                                                        ps[fs * 2 + th][:, :], ALU.add), "vd")
                            else:
                                ctx.inc(e.tensor_copy(out_sb[:, sl], ps[fs * 2 + th][:, :]), "vd")
                ctx.close_phase(e)

            @block.sync
            def _(e):
                ctx.open_phase(e)
                ctx.wait(e, "vd", vd_grp[-1])
                o3 = out_ap.rearrange("(f p) t -> p f t", p=128)
                s3 = out_sb[:, :].rearrange("p (f t) -> p f t", f=NKC)
                for c4 in range(4):
                    ctx.inc(e.dma_start(out=o3[:, 4 * c4:4 * c4 + 4, :],
                                        in_=s3[:, 4 * c4:4 * c4 + 4, :]), f"{tag}_S", 16)
                e.wait_ge(sS, 64)
                ctx.close_phase(e)

            @block.scalar
            def _(e):
                ctx.open_phase(e)
                ctx.close_phase(e)
    ctx.end_phase()


def ph_ffn(ctx, tag, xnloc_ap, parts, out_ap):
    nc = ctx.nc
    with ExitStack() as st:
        xn_sb = st.enter_context(nc.sbuf_tensor(f"{tag}_xn", [128, NKC * TL], BF16))
        h_sb = st.enter_context(nc.sbuf_tensor(f"{tag}_h", [128, 16 * TL], BF16))
        acc = st.enter_context(nc.sbuf_tensor(f"{tag}_acc", [128, NKC * TL], F32))
        gb = [st.enter_context(nc.sbuf_tensor(f"{tag}_g{i}", [128, NKC * 128], BF16)) for i in range(2)]
        ub = [st.enter_context(nc.sbuf_tensor(f"{tag}_u{i}", [128, NKC * 128], BF16)) for i in range(2)]
        db = [st.enter_context(nc.sbuf_tensor(f"{tag}_d{i}", [128, 16 * 512], BF16)) for i in range(2)]
        sg = st.enter_context(nc.sbuf_tensor(f"{tag}_sg", [128, 512], F32))
        ps = [st.enter_context(nc.psum_tensor(f"{tag}_ps{i}", [128, 512], F32)) for i in range(8)]
        sX = ctx.new_sem(f"{tag}_X")
        sGU = [ctx.new_sem(f"{tag}_GU{i}") for i in range(2)]
        sD = [ctx.new_sem(f"{tag}_D{i}") for i in range(2)]
        sS = ctx.new_sem(f"{tag}_S")

        td0, ad0, vd0, gd0 = ctx.val("td"), ctx.val("ad"), ctx.val("vd"), ctx.val("gd")
        quarters = []
        for pi, (_, _, _, n_it) in enumerate(parts):
            for off in range(0, n_it, 16):
                quarters.append((pi, off, min(16, n_it - off)))
        tdv, adv, vdv = td0, ad0, vd0
        td_gu, ad_silu, vd_h, td_dn, vd_dn = {}, {}, {}, {}, {}
        gu_seq = []
        dn_seq = []
        for qi, (pi, off, n_itq) in enumerate(quarters):
            for itl in range(n_itq):
                for th in range(2):
                    td_gu[(qi, itl, th)] = tdv + 1
                    tdv += 1
                    ad_silu[(qi, itl, th)] = adv + 1
                    adv += 1
                    vd_h[(qi, itl, th)] = vdv + 1
                    vdv += 1
                gu_seq.append((qi, itl))
            for fog in range(4):
                td_dn[(qi, fog)] = tdv + 1
                tdv += 1
                vd_dn[(qi, fog)] = vdv + 8
                vdv += 8
                dn_seq.append((qi, fog))
        vd_final = vdv + NKC
        td_end, ad_end, vd_end = tdv, adv, vdv

        with nc.Block() as block:
            @block.gpsimd
            def _(e):
                ctx.open_phase(e)
                e.memset(acc[:, :], 0.0)
                ctx.inc(e.nop(), "gd")
                xn3 = xnloc_ap.rearrange("(f p) t -> p f t", p=128)
                xd3 = xn_sb[:, :].rearrange("p (f t) -> p f t", f=NKC)
                for c4 in range(4):
                    ctx.inc(e.dma_start(out=xd3[:, 4 * c4:4 * c4 + 4, :],
                                        in_=xn3[:, 4 * c4:4 * c4 + 4, :]), f"{tag}_X", 16)
                gu_i = 0
                dn_i = 0
                for qi, (pi, off, n_itq) in enumerate(quarters):
                    gT_ap, uT_ap, dT_ap, n_it = parts[pi]
                    g3 = gT_ap.rearrange("(kc p) n -> p kc n", p=128)
                    u3 = uT_ap.rearrange("(kc p) n -> p kc n", p=128)
                    d3 = dT_ap.rearrange("(i p) c -> p i c", p=128)
                    for itl in range(n_itq):
                        it = off + itl
                        if gu_i >= 2:
                            pqi, pitl = gu_seq[gu_i - 2]
                            ctx.wait(e, "td", td_gu[(pqi, pitl, 1)])
                        ctx.inc(e.dma_start(out=gb[gu_i % 2][:, :].rearrange("p (kc c) -> p kc c", kc=NKC),
                                            in_=g3[:, :, 128 * it:128 * (it + 1)]), f"{tag}_GU{gu_i % 2}", 16)
                        ctx.inc(e.dma_start(out=ub[gu_i % 2][:, :].rearrange("p (kc c) -> p kc c", kc=NKC),
                                            in_=u3[:, :, 128 * it:128 * (it + 1)]), f"{tag}_GU{gu_i % 2}", 16)
                        gu_i += 1
                    for fog in range(4):
                        if dn_i >= 2:
                            pqi, pfog = dn_seq[dn_i - 2]
                            ctx.wait(e, "td", td_dn[(pqi, pfog)])
                        ctx.inc(e.dma_start(
                            out=db[dn_i % 2][:, 0:n_itq * 512].rearrange("p (i c) -> p i c", i=n_itq),
                            in_=d3[:, off:off + n_itq, 512 * fog:512 * (fog + 1)]), f"{tag}_D{dn_i % 2}", 16)
                        dn_i += 1
                ctx.close_phase(e)

            @block.tensor
            def _(e):
                ctx.open_phase(e)
                e.wait_ge(sX, 64)
                gu_i = 0
                dn_i = 0
                for qi, (pi, off, n_itq) in enumerate(quarters):
                    for itl in range(n_itq):
                        e.wait_ge(sGU[gu_i % 2], 32 * (gu_i // 2 + 1))
                        for th in range(2):
                            if qi > 0 or itl > 0:
                                pass
                            key_prev = None
                            if itl >= 1:
                                key_prev = (qi, itl - 1, th)
                            elif qi >= 1:
                                ppi, poff, pn = quarters[qi - 1]
                                key_prev = "down"
                            if key_prev == "down":
                                ctx.wait(e, "vd", vd_dn[(qi - 1, 3)])
                            elif key_prev is not None:
                                ctx.wait(e, "vd", vd_h[key_prev])
                            for kc in range(NKC):
                                e.matmul(ps[th][:, :], gb[gu_i % 2][:, kc * 128:(kc + 1) * 128],
                                         xn_sb[:, kc * TL + 512 * th:kc * TL + 512 * (th + 1)],
                                         start=(kc == 0), stop=(kc == NKC - 1))
                            for kc in range(NKC):
                                mm = e.matmul(ps[2 + th][:, :], ub[gu_i % 2][:, kc * 128:(kc + 1) * 128],
                                              xn_sb[:, kc * TL + 512 * th:kc * TL + 512 * (th + 1)],
                                              start=(kc == 0), stop=(kc == NKC - 1))
                            ctx.inc(mm, "td")
                        gu_i += 1
                    for fog in range(4):
                        e.wait_ge(sD[dn_i % 2], 16 * (dn_i // 2 + 1))
                        if fog == 0:
                            ctx.wait(e, "vd", vd_h[(qi, n_itq - 1, 1)])
                            if qi >= 1:
                                ctx.wait(e, "vd", vd_dn[(qi - 1, 3)])
                        else:
                            ctx.wait(e, "vd", vd_dn[(qi, fog - 1)])
                        for itl in range(n_itq):
                            for fs in range(4):
                                for th in range(2):
                                    mm = e.matmul(ps[fs * 2 + th][:, :],
                                                  db[dn_i % 2][:, itl * 512 + 128 * fs:itl * 512 + 128 * (fs + 1)],
                                                  h_sb[:, itl * TL + 512 * th:itl * TL + 512 * (th + 1)],
                                                  start=(itl == 0), stop=(itl == n_itq - 1))
                        ctx.inc(mm, "td")
                        dn_i += 1
                assert ctx.val("td") == td_end
                ctx.close_phase(e)

            @block.scalar
            def _(e):
                ctx.open_phase(e)
                for qi, (pi, off, n_itq) in enumerate(quarters):
                    for itl in range(n_itq):
                        for th in range(2):
                            ctx.wait(e, "td", td_gu[(qi, itl, th)])
                            if (qi, itl) != (0, 0) or th != 0:
                                pass
                            if not (qi == 0 and itl == 0 and th == 0):
                                prev = (qi, itl, th - 1) if th == 1 else ((qi, itl - 1, 1) if itl > 0 else (qi - 1, quarters[qi - 1][2] - 1, 1))
                                ctx.wait(e, "vd", vd_h[prev])
                            ctx.inc(e.activation(sg[:, :], ps[th][:, :], AF.Silu), "ad")
                ctx.close_phase(e)

            @block.vector
            def _(e):
                ctx.open_phase(e)
                for qi, (pi, off, n_itq) in enumerate(quarters):
                    for itl in range(n_itq):
                        for th in range(2):
                            ctx.wait(e, "ad", ad_silu[(qi, itl, th)])
                            ctx.inc(e.tensor_tensor(h_sb[:, itl * TL + 512 * th:itl * TL + 512 * (th + 1)],
                                                    sg[:, :], ps[2 + th][:, :], ALU.mult), "vd")
                    for fog in range(4):
                        ctx.wait(e, "td", td_dn[(qi, fog)])
                        if qi == 0 and fog == 0:
                            ctx.wait(e, "gd", ctx.val("gd"))
                        for fs in range(4):
                            fo = 4 * fog + fs
                            for th in range(2):
                                ctx.inc(e.tensor_tensor(acc[:, fo * TL + 512 * th:fo * TL + 512 * (th + 1)],
                                                        acc[:, fo * TL + 512 * th:fo * TL + 512 * (th + 1)],
                                                        ps[fs * 2 + th][:, :], ALU.add), "vd")
                assert ctx.val("vd") == vd_end
                ctx.close_phase(e)

            @block.sync
            def _(e):
                ctx.open_phase(e)
                ctx.wait(e, "vd", vd_end)
                o3 = out_ap.rearrange("(f p) t -> p f t", p=128)
                a3 = acc[:, :].rearrange("p (f t) -> p f t", f=NKC)
                for c4 in range(4):
                    ctx.inc(e.dma_start(out=o3[:, 4 * c4:4 * c4 + 4, :],
                                        in_=a3[:, 4 * c4:4 * c4 + 4, :]), f"{tag}_S", 16)
                e.wait_ge(sS, 64)
                ctx.close_phase(e)
    ctx.end_phase()


def ph_ghost(ctx, tag, idx_aps, lut_ap, agin_aps, agout_aps):
    nc = ctx.nc
    aux = ctx.aux
    parts = [("g", 256, INTER_G), ("u", 256, INTER_G), ("d", 688, C)]
    with ExitStack() as st:
        lutrep = st.enter_context(nc.sbuf_tensor(f"{tag}_lut", [128, 16], F32))
        lrow = st.enter_context(nc.sbuf_tensor(f"{tag}_lr", [1, 16], F32))
        p_lut = st.enter_context(nc.psum_tensor(f"{tag}_pl", [128, 16], F32))
        idx_sb = st.enter_context(nc.sbuf_tensor(f"{tag}_idx", [128, INTER_G], BF16))
        w_sb = st.enter_context(nc.sbuf_tensor(f"{tag}_w", [128, INTER_G], BF16))
        eq_sb = st.enter_context(nc.sbuf_tensor(f"{tag}_eq", [128, INTER_G], BF16))
        sL = ctx.new_sem(f"{tag}_L")
        sP = ctx.new_sem(f"{tag}_P")
        sC = ctx.new_sem(f"{tag}_C")

        tiles = []
        for key, nrows, ncols in parts:
            r = 0
            while r < nrows:
                tiles.append((key, r, min(128, nrows - r), ncols))
                r += 128
        ntiles = len(tiles)

        td0, vd0 = ctx.val("td"), ctx.val("vd")
        td_lut = td0 + 1
        vd_tile = [vd0 + 1 + (t + 1) for t in range(ntiles)]

        sI = ctx.new_sem(f"{tag}_I")
        with nc.Block() as block:
            @block.sync
            def _(e):
                ctx.open_phase(e)
                ctx.inc(e.dma_start(out=lrow[:, :], in_=lut_ap), f"{tag}_L", 16)
                for t, (key, r0, nr, ncols) in enumerate(tiles):
                    if t >= 1:
                        pk, pr0, pnr, pnc = tiles[t - 1]
                        ctx.wait(e, "vd", vd_tile[t - 1])
                        ctx.inc(e.dma_start(out=agin_aps[pk][pr0:pr0 + pnr, :], in_=w_sb[0:pnr, 0:pnc]), f"{tag}_P", 16)
                pk, pr0, pnr, pnc = tiles[-1]
                ctx.wait(e, "vd", vd_tile[-1])
                ctx.inc(e.dma_start(out=agin_aps[pk][pr0:pr0 + pnr, :], in_=w_sb[0:pnr, 0:pnc]), f"{tag}_P", 16)
                e.wait_ge(sP, 16 * ntiles)
                ctx.close_phase(e)

            @block.tensor
            def _(e):
                ctx.open_phase(e)
                e.wait_ge(sL, 16)
                mm = e.matmul(p_lut[:, :], aux["onesr"][:, :], lrow[:, :], start=True, stop=True)
                ctx.inc(mm, "td")
                ctx.close_phase(e)

            @block.vector
            def _(e):
                ctx.open_phase(e)
                ctx.wait(e, "td", td_lut)
                ctx.inc(e.tensor_copy(lutrep[:, :], p_lut[:, :]), "vd")
                for t, (key, r0, nr, ncols) in enumerate(tiles):
                    e.wait_ge(sI, 16 * (t + 1))
                    if t >= 1:
                        e.wait_ge(sP, 16 * t)
                    for b in range(16):
                        e.tensor_scalar(eq_sb[0:nr, 0:ncols], idx_sb[0:nr, 0:ncols],
                                        float(b), lutrep[0:nr, b:b + 1], ALU.is_equal, ALU.mult)
                        if b == 0:
                            tt = e.tensor_scalar(w_sb[0:nr, 0:ncols], eq_sb[0:nr, 0:ncols], 1.0, None, ALU.mult)
                        else:
                            tt = e.tensor_tensor(w_sb[0:nr, 0:ncols], w_sb[0:nr, 0:ncols],
                                                 eq_sb[0:nr, 0:ncols], ALU.add)
                    ctx.inc(tt, "vd")
                assert ctx.val("vd") == vd_tile[-1]
                ctx.close_phase(e)

            @block.scalar
            def _(e):
                ctx.open_phase(e)
                ctx.close_phase(e)

            @block.gpsimd
            def _(e):
                ctx.open_phase(e)
                for t, (key, r0, nr, ncols) in enumerate(tiles):
                    if t >= 1:
                        ctx.wait(e, "vd", vd_tile[t - 1])
                    ctx.inc(e.dma_start(out=idx_sb[0:nr, 0:ncols], in_=idx_aps[key][r0:r0 + nr, :]), f"{tag}_I", 16)
                e.wait_ge(sP, 16 * ntiles)
                ctx.fence(e, tag)
                for key, nrows, ncols in parts:
                    cc = e.collective_compute("AllGather", ALU.bypass, replica_groups=RG,
                                              ins=[agin_aps[key]], outs=[agout_aps[key]])
                    ctx.inc(cc, f"{tag}_C")
                e.wait_ge(sC, ctx.val(f"{tag}_C"))
                ctx.close_phase(e)
    ctx.end_phase()


def ph_qkv_scan1(ctx, tag, xnall_ap, w1_ap, dec_ap, agin_ap, agout_ap,
                 xnloc_ap=None, fparts=None, ffn_out_ap=None):
    nc = ctx.nc
    aux = ctx.aux
    CHK = 512
    NTC = T // CHK
    MDS = [2, 3, 0, 1, 4, 5]
    proj_seq = [(tc, md) for tc in range(NTC) for md in MDS]
    n_slots = len(proj_seq)

    quarters = []
    ffn_units = []
    if fparts is not None:
        for pi, (_, _, _, n_it) in enumerate(fparts):
            for off in range(0, n_it, 8):
                quarters.append((pi, off, min(8, n_it - off)))
        for qi, (pi, off, n_itq) in enumerate(quarters):
            for itl in range(n_itq):
                for th in range(2):
                    ffn_units.append(("gu", qi, itl, th))
            for fog in range(4):
                for fs in range(4):
                    for th in range(2):
                        ffn_units.append(("dn", qi, fog, fs, th))

    def dn_unit_index(qi, fog, fs, th):
        return (qi * 4 + fog) * 8 + fs * 2 + th

    def prev_pair(tc, h):
        return (tc, h - 1) if h > 0 else (tc - 1, NH1 - 1)

    with ExitStack() as st:
        w1_sb = st.enter_context(nc.sbuf_tensor(f"{tag}_w1", [128, NKC * 768], BF16))
        xnb = [st.enter_context(nc.sbuf_tensor(f"{tag}_xn{i}", [128, NKC * 512], BF16)) for i in range(2)]
        dtile = st.enter_context(nc.sbuf_tensor(f"{tag}_dt", [128, NH1 * CHK], F32))
        dcol = st.enter_context(nc.sbuf_tensor(f"{tag}_dc", [128, NH1], F32))
        drow = st.enter_context(nc.sbuf_tensor(f"{tag}_dr", [1, 8], F32))
        phik = st.enter_context(nc.sbuf_tensor(f"{tag}_pk", [128, NH1 * CHK], F32))
        qphi = st.enter_context(nc.sbuf_tensor(f"{tag}_qp", [128, NH1 * CHK], F32))
        Xb = st.enter_context(nc.sbuf_tensor(f"{tag}_X", [128, NH1 * CHK], F32))
        skv = [[st.enter_context(nc.sbuf_tensor(f"{tag}_skv{h}{i}", [128, CHK], F32))
                for i in range(2)] for h in range(NH1)]
        sks = [[st.enter_context(nc.sbuf_tensor(f"{tag}_sks{h}{i}", [128, CHK], F32))
                for i in range(2)] for h in range(NH1)]
        nm = st.enter_context(nc.sbuf_tensor(f"{tag}_nm", [128, NH1 * CHK], F32))
        tmin = st.enter_context(nc.sbuf_tensor(f"{tag}_tmin", [128, 512], F32))
        texp = st.enter_context(nc.sbuf_tensor(f"{tag}_texp", [128, 512], F32))
        tmax = st.enter_context(nc.sbuf_tensor(f"{tag}_tmax", [128, 512], F32))
        denr = st.enter_context(nc.sbuf_tensor(f"{tag}_den", [1, CHK], F32))
        o1sb = [st.enter_context(nc.sbuf_tensor(f"{tag}_o{i}", [128, CHK], BF16)) for i in range(2)]
        pr = [st.enter_context(nc.psum_tensor(f"{tag}_pr{i}", [128, 512], F32)) for i in range(2)]
        den_ps = st.enter_context(nc.psum_tensor(f"{tag}_dps", [1, 512], F32))
        rb_ps = st.enter_context(nc.psum_tensor(f"{tag}_rb", [128, 512], F32))
        db_ps = st.enter_context(nc.psum_tensor(f"{tag}_db", [128, NH1], F32))
        sDec = ctx.new_sem(f"{tag}_dec")
        sW = ctx.new_sem(f"{tag}_W")
        sXn = [ctx.new_sem(f"{tag}_Xn{i}") for i in range(2)]
        sO = [ctx.new_sem(f"{tag}_O{i}") for i in range(2)]
        sC = ctx.new_sem(f"{tag}_C")
        if fparts is not None:
            fxn_sb = st.enter_context(nc.sbuf_tensor(f"{tag}_fxn", [128, NKC * TL], BF16))
            fh_sb = st.enter_context(nc.sbuf_tensor(f"{tag}_fh", [128, 8 * TL], BF16))
            fgb = [st.enter_context(nc.sbuf_tensor(f"{tag}_fg{i}", [128, NKC * 128], BF16)) for i in range(2)]
            fub = [st.enter_context(nc.sbuf_tensor(f"{tag}_fu{i}", [128, NKC * 128], BF16)) for i in range(2)]
            fdb = [st.enter_context(nc.sbuf_tensor(f"{tag}_fd{i}", [128, 8 * 512], BF16)) for i in range(2)]
            fsg = st.enter_context(nc.sbuf_tensor(f"{tag}_fsg", [128, 512], F32))
            fstg = [st.enter_context(nc.sbuf_tensor(f"{tag}_fst{i}", [128, 512], F32)) for i in range(4)]
            f_ps = [st.enter_context(nc.psum_tensor(f"{tag}_fps{i}", [128, 512], F32)) for i in range(2)]
            sFX = ctx.new_sem(f"{tag}_FX")
            sGU = [ctx.new_sem(f"{tag}_fGU{i}") for i in range(2)]
            sD = [ctx.new_sem(f"{tag}_fD{i}") for i in range(2)]
            sAccR = [ctx.new_sem(f"{tag}_fA{i}") for i in range(4)]

        E = {}
        posl = {}

        def pe_walk(emit, e=None):
            td = ctx.val("td")
            fu = [0]
            gb_i = [0]
            per_slot = (len(ffn_units) / n_slots) if ffn_units else 0.0

            def mm_inc(key, slot=None):
                nonlocal td
                td += 1
                if not emit:
                    E[key] = td
                    if slot is not None:
                        posl[key] = slot

            def fire(mm, key, slot=None):
                if emit:
                    ctx.cnt["td"] += 1
                    mm.then_inc(ctx.sem("td"), 1)
                else:
                    mm_inc(key, slot)

            def wait(k, v):
                if emit and v > 0:
                    ctx.wait(e, k, v)

            def wait_h(sem, v):
                if emit:
                    e.wait_ge(sem, v)

            def emit_ffn_unit(slot):
                u = ffn_units[fu[0]]
                if u[0] == "gu":
                    _, qi, itl, th = u
                    g_i = gb_i[0]
                    if fu[0] == 0:
                        wait_h(sFX, 64)
                    if th == 0:
                        wait_h(sGU[g_i % 2], 32 * (g_i // 2 + 1))
                    prev = None
                    if (qi, itl, th) != (0, 0, 0):
                        if th == 1:
                            prev = (qi, itl, 0)
                        elif itl > 0:
                            prev = (qi, itl - 1, 1)
                        else:
                            prev = (qi - 1, quarters[qi - 1][2] - 1, 1)
                    if prev is not None:
                        wait("ad", E.get(("silu",) + prev, 0))
                        wait("vd", E.get(("hmul",) + prev, 0))
                    if itl == 0 and th == 0 and qi > 0:
                        wait("vd", E.get(("dnev", qi - 1, 3, 3, 1), 0))
                    mm = None
                    for kc in range(NKC):
                        if emit:
                            e.matmul(f_ps[0][:, :], fgb[g_i % 2][:, kc * 128:(kc + 1) * 128],
                                     fxn_sb[:, kc * TL + 512 * th:kc * TL + 512 * (th + 1)],
                                     start=(kc == 0), stop=(kc == NKC - 1))
                    for kc in range(NKC):
                        if emit:
                            mm = e.matmul(f_ps[1][:, :], fub[g_i % 2][:, kc * 128:(kc + 1) * 128],
                                          fxn_sb[:, kc * TL + 512 * th:kc * TL + 512 * (th + 1)],
                                          start=(kc == 0), stop=(kc == NKC - 1))
                    fire(mm, ("gumm", qi, itl, th), slot)
                    if th == 1:
                        gb_i[0] += 1
                else:
                    _, qi, fog, fs, th = u
                    pi, off, n_itq = quarters[qi]
                    dn_i = qi * 4 + fog
                    if fs == 0 and th == 0:
                        wait_h(sD[dn_i % 2], 16 * (dn_i // 2 + 1))
                        if fog == 0:
                            lastgu = (qi, n_itq - 1, 1)
                            wait("ad", E.get(("silu",) + lastgu, 0))
                            wait("vd", E.get(("hmul",) + lastgu, 0))
                    un = dn_unit_index(qi, fog, fs, th)
                    if un >= 2:
                        wait("vd", E.get(("dnev_u", un - 2), 0))
                    mm = None
                    for itl in range(n_itq):
                        if emit:
                            mm = e.matmul(f_ps[un % 2][:, :],
                                          fdb[dn_i % 2][:, itl * 512 + 128 * fs:itl * 512 + 128 * (fs + 1)],
                                          fh_sb[:, itl * TL + 512 * th:itl * TL + 512 * (th + 1)],
                                          start=(itl == 0), stop=(itl == n_itq - 1))
                    fire(mm, ("dnmm", qi, fog, fs, th), slot)
                fu[0] += 1

            def emit_tail(tc):
                for h in range(NH1):
                    wait("vd", E.get(("ds", tc, h), 0))
                    if (tc, h) != (0, 0):
                        wait("vd", E.get(("clip",) + prev_pair(tc, h), 0))
                    mm = None
                    if emit:
                        mm = e.matmul(den_ps[0:1, :], aux["onesc"][:, :],
                                      Xb[:, CHK * h:CHK * (h + 1)], start=True, stop=True)
                    fire(mm, ("den", tc, h))
                    wait("ad", E.get(("recip", tc, h), 0))
                    if (tc, h) != (0, 0):
                        wait("vd", E.get(("out",) + prev_pair(tc, h), 0))
                    if emit:
                        mm = e.matmul(rb_ps[:, :], aux["onesr"][:, :],
                                      denr[0:1, :], start=True, stop=True)
                    fire(mm, ("rb", tc, h))

            wait("ad", E.get("dec3", 0))
            mm = None
            if emit:
                mm = e.matmul(db_ps[:, :], aux["onesr"][:, :], drow[0:1, 6:6 + NH1],
                              start=True, stop=True)
            fire(mm, "dbc")
            wait_h(sW, 64)
            pending = None
            for si, (tc, md) in enumerate(proj_seq):
                if md == MDS[0]:
                    wait_h(sXn[tc % 2], 64 * (tc // 2 + 1))
                while fu[0] < len(ffn_units) and fu[0] < per_slot * (si + 1):
                    emit_ffn_unit(si)
                if si >= 2:
                    ptc, pmd = proj_seq[si - 2]
                    pk = ("min", ptc, pmd) if pmd < 4 else ("X", ptc, pmd)
                    wait("vd", E.get(pk, 0))
                mm = None
                for kc in range(NKC):
                    if emit:
                        mm = e.matmul(pr[si % 2][:, :],
                                      w1_sb[:, kc * 768 + 128 * md:kc * 768 + 128 * (md + 1)],
                                      xnb[tc % 2][:, kc * 512:(kc + 1) * 512],
                                      start=(kc == 0), stop=(kc == NKC - 1))
                fire(mm, ("proj", tc, md), si)
                if md == MDS[-1]:
                    if pending is not None:
                        emit_tail(pending)
                    pending = tc
            while fu[0] < len(ffn_units):
                emit_ffn_unit(n_slots - 1)
            if pending is not None:
                emit_tail(pending)
            if not emit:
                E["td_end"] = td

        def act_walk(emit, e=None):
            ad = ctx.val("ad")
            fu = [0]
            silu_seq = [u for u in ffn_units if u[0] == "gu"]

            def fire(a, key, n=1):
                nonlocal ad
                ad += n
                if emit:
                    ctx.cnt["ad"] += n
                    a.then_inc(ctx.sem("ad"), n)
                else:
                    E[key] = ad

            def wait(k, v):
                if emit and v > 0:
                    ctx.wait(e, k, v)

            def weave_silu(limit_slot):
                while fu[0] < len(silu_seq):
                    _, qi, itl, th = silu_seq[fu[0]]
                    if posl.get(("gumm", qi, itl, th), 10 ** 9) > limit_slot:
                        break
                    wait("td", E[("gumm", qi, itl, th)])
                    prev = None
                    if (qi, itl, th) != (0, 0, 0):
                        if th == 1:
                            prev = (qi, itl, 0)
                        elif itl > 0:
                            prev = (qi, itl - 1, 1)
                        else:
                            prev = (qi - 1, quarters[qi - 1][2] - 1, 1)
                    if prev is not None:
                        wait("vd", E.get(("hmul",) + prev, 0))
                    a = None
                    if emit:
                        a = e.activation(fsg[:, :], f_ps[0][:, :], AF.Silu)
                    fire(a, ("silu", qi, itl, th))
                    fu[0] += 1

            if emit:
                e.wait_ge(sDec, 16)
            a = None
            if emit:
                a = e.activation(drow[0:1, 2:2 + NH1], drow[0:1, 0:NH1], AF.Exp, scale=-1.0)
            fire(a, "dec1")
            wait("vd", E.get("decadd", 0))
            if emit:
                a = e.activation(drow[0:1, 4:4 + NH1], drow[0:1, 2:2 + NH1], AF.Ln)
            fire(a, "dec2")
            if emit:
                a = e.activation(drow[0:1, 6:6 + NH1], drow[0:1, 4:4 + NH1], AF.Exp, scale=-1.0)
            fire(a, "dec3")
            for tc in range(NTC):
                for mdi, md in enumerate(MDS):
                    si = 6 * tc + mdi
                    if mdi == 4:
                        weave_silu(6 * tc + 5)
                        if tc >= 1:
                            for h in range(NH1):
                                wait("vd", E.get(("clip", tc - 1, h), 0))
                                a = None
                                if emit:
                                    a = e.activation(denr[0:1, :], denr[0:1, :], AF.Ln)
                                fire(a, ("lnr", tc - 1, h))
                                if emit:
                                    a = e.activation(denr[0:1, :], denr[0:1, :], AF.Exp, scale=-1.0)
                                fire(a, ("recip", tc - 1, h))
                    weave_silu(si + 1)
                    if md < 4:
                        wait("vd", E.get(("min", tc, md), 0))
                        a = None
                        if emit:
                            a = e.activation(texp[:, :], tmin[:, :], AF.Exp)
                        fire(a, ("exp", tc, md))
            weave_silu(10 ** 9)
            for h in range(NH1):
                wait("vd", E.get(("clip", NTC - 1, h), 0))
                a = None
                if emit:
                    a = e.activation(denr[0:1, :], denr[0:1, :], AF.Ln)
                fire(a, ("lnr", NTC - 1, h))
                if emit:
                    a = e.activation(denr[0:1, :], denr[0:1, :], AF.Exp, scale=-1.0)
                fire(a, ("recip", NTC - 1, h))
            if not emit:
                E["ad_end"] = ad

        def dve_walk(emit, e=None):
            vd = ctx.val("vd")
            hm = [0]

            def fire(tt, key):
                nonlocal vd
                vd += 1
                if emit:
                    ctx.cnt["vd"] += 1
                    tt.then_inc(ctx.sem("vd"), 1)
                else:
                    E[key] = vd

            def wait(k, v):
                if emit and v > 0:
                    ctx.wait(e, k, v)

            def weave(limit_slot):
                while hm[0] < len(ffn_units):
                    u = ffn_units[hm[0]]
                    if u[0] == "gu":
                        _, qi, itl, th = u
                        if posl.get(("gumm", qi, itl, th), 10 ** 9) > limit_slot:
                            break
                        wait("ad", E[("silu", qi, itl, th)])
                        tt = None
                        if emit:
                            tt = e.tensor_tensor(fh_sb[:, itl * TL + 512 * th:itl * TL + 512 * (th + 1)],
                                                 fsg[:, :], f_ps[1][:, :], ALU.mult)
                        fire(tt, ("hmul", qi, itl, th))
                    else:
                        _, qi, fog, fs, th = u
                        if posl.get(("dnmm", qi, fog, fs, th), 10 ** 9) > limit_slot:
                            break
                        un = dn_unit_index(qi, fog, fs, th)
                        wait("td", E[("dnmm", qi, fog, fs, th)])
                        tt = None
                        if emit:
                            if un >= 4:
                                e.wait_ge(sAccR[un % 4], 16 * (un // 4))
                            tt = e.tensor_scalar(fstg[un % 4][:, :], f_ps[un % 2][:, :], 1.0, None, ALU.mult)
                        fire(tt, ("dnev_u", un))
                        if not emit:
                            E[("dnev", qi, fog, fs, th)] = vd
                    hm[0] += 1

            def finalize(tc):
                for h in range(NH1):
                    gh = tc * NH1 + h
                    pp = tc % 2
                    wait("td", E.get(("den", tc, h), 0))
                    tt = None
                    if emit:
                        tt = e.tensor_scalar(denr[0:1, :], den_ps[0:1, :], DEN_EPS, None, ALU.max)
                    fire(tt, ("clip", tc, h))
                    if emit and gh >= 2:
                        e.wait_ge(sO[gh % 2], 16 * (gh // 2))
                    wait("td", E.get(("rb", tc, h), 0))
                    if emit:
                        tt = e.tensor_tensor(o1sb[gh % 2][:, :], nm[:, CHK * h:CHK * (h + 1)],
                                             rb_ps[:, :], ALU.mult)
                    fire(tt, ("out", tc, h))

            wait("ad", E.get("dec1", 0))
            tt = None
            if emit:
                tt = e.tensor_scalar(drow[0:1, 2:2 + NH1], drow[0:1, 2:2 + NH1], 1.0, None, ALU.add)
            fire(tt, "decadd")
            wait("td", E.get("dbc", 0))
            if emit:
                tt = e.tensor_scalar(dcol[:, :], db_ps[:, :], 1.0, None, ALU.mult)
            fire(tt, "dcol")
            if emit:
                ctx.wait(e, "gd", ctx.val("gd"))
            for h in range(NH1):
                if emit:
                    tt = e.tensor_scalar(dtile[:, CHK * h:CHK * (h + 1)], dtile[:, CHK * h:CHK * (h + 1)],
                                         dcol[:, h:h + 1], None, ALU.mult)
                fire(tt, ("dtile", h))
            for tc in range(NTC):
                for mdi, md in enumerate(MDS):
                    si = 6 * tc + mdi
                    if mdi == 4:
                        weave(6 * tc + 5)
                        if tc >= 1:
                            finalize(tc - 1)
                    weave(si + 1)
                    pp = si % 2
                    wait("td", E[("proj", tc, md)])
                    if md < 4:
                        h = md % 2
                        is_k = md >= 2
                        dst = (phik if is_k else qphi)[:, CHK * h:CHK * (h + 1)]
                        if emit:
                            e.tensor_scalar(tmax[:, :], pr[pp][:, :], 0.0, None, ALU.max)
                            tt = e.tensor_scalar(tmin[:, :], pr[pp][:, :], 0.0, None, ALU.min)
                        fire(tt, ("min", tc, md))
                        wait("ad", E[("exp", tc, md)])
                        if emit:
                            tt = e.tensor_tensor(dst, texp[:, :], tmax[:, :], ALU.add)
                        fire(tt, ("phi", tc, md))
                    else:
                        h = md % 2
                        if tc >= 1:
                            wait("td", E[("den", tc - 1, h)])
                        if emit:
                            tt = e.tensor_tensor(Xb[:, CHK * h:CHK * (h + 1)],
                                                 phik[:, CHK * h:CHK * (h + 1)], pr[pp][:, :], ALU.mult)
                        fire(tt, ("X", tc, md))
                for h in range(NH1):
                    pp = tc % 2
                    init_kv = 0.0 if tc == 0 else skv[h][1 - pp][:, CHK - 1:CHK]
                    init_ks = 0.0 if tc == 0 else sks[h][1 - pp][:, CHK - 1:CHK]
                    tt = None
                    if emit:
                        tt = e.tensor_tensor_scan(skv[h][pp][:, :], dtile[:, CHK * h:CHK * (h + 1)],
                                                  Xb[:, CHK * h:CHK * (h + 1)], init_kv,
                                                  ALU.mult, ALU.add)
                    fire(tt, ("scankv", tc, h))
                    if emit:
                        tt = e.tensor_tensor_scan(sks[h][pp][:, :], dtile[:, CHK * h:CHK * (h + 1)],
                                                  phik[:, CHK * h:CHK * (h + 1)], init_ks,
                                                  ALU.mult, ALU.add)
                    fire(tt, ("scanks", tc, h))
                    if emit:
                        tt = e.tensor_tensor(Xb[:, CHK * h:CHK * (h + 1)],
                                             qphi[:, CHK * h:CHK * (h + 1)], sks[h][pp][:, :], ALU.mult)
                    fire(tt, ("ds", tc, h))
                    if emit:
                        tt = e.tensor_tensor(nm[:, CHK * h:CHK * (h + 1)],
                                             qphi[:, CHK * h:CHK * (h + 1)], skv[h][pp][:, :], ALU.mult)
                    fire(tt, ("nm", tc, h))
            weave(10 ** 9)
            finalize(NTC - 1)
            if not emit:
                E["vd_end"] = vd

        pe_walk(emit=False)
        act_walk(emit=False)
        dve_walk(emit=False)
        globals()["SC1_DBG"] = (dict(E), dict(posl))

        with nc.Block() as block:
            @block.sync
            def _(e):
                ctx.open_phase(e)
                ctx.inc(e.dma_start(out=drow[0:1, 0:NH1], in_=dec_ap), f"{tag}_dec", 16)
                srcv = xnall_ap.rearrange("(b f p) t -> b p f t", b=NCORES, p=128)
                ag3 = agin_ap.rearrange("(r p) t -> p r t", p=128)
                for tc in range(NTC):
                    if tc >= 2:
                        ctx.wait(e, "td", E[("proj", tc - 2, MDS[-1])])
                    blk = tc // 2
                    off = 512 * (tc % 2)
                    dst = xnb[tc % 2][:, :].rearrange("p (f t) -> p f t", f=NKC)
                    for c4 in range(4):
                        ctx.inc(e.dma_start(out=dst[:, 4 * c4:4 * c4 + 4, :],
                                            in_=srcv[blk][:, 4 * c4:4 * c4 + 4, off:off + 512]),
                                f"{tag}_Xn{tc % 2}", 16)
                    if tc >= 2:
                        stc = tc - 2
                        for h in range(NH1):
                            gh = stc * NH1 + h
                            ctx.wait(e, "vd", E[("out", stc, h)])
                            ctx.inc(e.dma_start(out=ag3[:, h, CHK * stc:CHK * (stc + 1)],
                                                in_=o1sb[gh % 2][:, :]), f"{tag}_O{gh % 2}", 16)
                for stc in (NTC - 2, NTC - 1):
                    for h in range(NH1):
                        gh = stc * NH1 + h
                        ctx.wait(e, "vd", E[("out", stc, h)])
                        ctx.inc(e.dma_start(out=ag3[:, h, CHK * stc:CHK * (stc + 1)],
                                            in_=o1sb[gh % 2][:, :]), f"{tag}_O{gh % 2}", 16)
                e.wait_ge(sO[0], 16 * (NTC * NH1 // 2))
                e.wait_ge(sO[1], 16 * (NTC * NH1 // 2))
                ctx.close_phase(e)

            @block.gpsimd
            def _(e):
                ctx.open_phase(e)
                e.memset(dtile[:, :], 1.0)
                ctx.inc(e.nop(), "gd")
                w3 = w1_ap.rearrange("(kc p) n -> p kc n", p=128)
                d3 = w1_sb[:, :].rearrange("p (kc n) -> p kc n", kc=NKC)
                for c4 in range(4):
                    ctx.inc(e.dma_start(out=d3[:, 4 * c4:4 * c4 + 4, :],
                                        in_=w3[:, 4 * c4:4 * c4 + 4, :]), f"{tag}_W", 16)
                if fparts is not None:
                    xn3 = xnloc_ap.rearrange("(f p) t -> p f t", p=128)
                    xd3 = fxn_sb[:, :].rearrange("p (f t) -> p f t", f=NKC)
                    for c4 in range(4):
                        ctx.inc(e.dma_start(out=xd3[:, 4 * c4:4 * c4 + 4, :],
                                            in_=xn3[:, 4 * c4:4 * c4 + 4, :]), f"{tag}_FX", 16)
                    gu_seq = []
                    for qi, (pi, off, n_itq) in enumerate(quarters):
                        for itl in range(n_itq):
                            gu_seq.append((qi, itl))
                    items = []
                    for k, (qi, itl) in enumerate(gu_seq):
                        items.append((posl[("gumm", qi, itl, 0)], 0, ("guload", k)))
                    for qi, (pi, off, n_itq) in enumerate(quarters):
                        for fog in range(4):
                            items.append((posl[("dnmm", qi, fog, 0, 0)], 1, ("dnload", qi, fog)))
                            for fs in range(4):
                                for th in range(2):
                                    items.append((posl[("dnmm", qi, fog, fs, th)] + 1, 2,
                                                  ("accum", qi, fog, fs, th)))
                    items.sort(key=lambda x: (x[0], x[1]))
                    for _, _, it_ in items:
                        if it_[0] == "guload":
                            k = it_[1]
                            qi, itl = gu_seq[k]
                            pi, off, n_itq = quarters[qi]
                            gT_ap, uT_ap, dT_ap, n_it = fparts[pi]
                            g3 = gT_ap.rearrange("(kc p) n -> p kc n", p=128)
                            u3 = uT_ap.rearrange("(kc p) n -> p kc n", p=128)
                            it = off + itl
                            if k >= 2:
                                pq, pitl = gu_seq[k - 2]
                                ctx.wait(e, "td", E[("gumm", pq, pitl, 1)])
                            ctx.inc(e.dma_start(out=fgb[k % 2][:, :].rearrange("p (kc c) -> p kc c", kc=NKC),
                                                in_=g3[:, :, 128 * it:128 * (it + 1)]), f"{tag}_fGU{k % 2}", 16)
                            ctx.inc(e.dma_start(out=fub[k % 2][:, :].rearrange("p (kc c) -> p kc c", kc=NKC),
                                                in_=u3[:, :, 128 * it:128 * (it + 1)]), f"{tag}_fGU{k % 2}", 16)
                        elif it_[0] == "dnload":
                            qi, fog = it_[1], it_[2]
                            pi, off, n_itq = quarters[qi]
                            gT_ap, uT_ap, dT_ap, n_it = fparts[pi]
                            d3 = dT_ap.rearrange("(i p) c -> p i c", p=128)
                            dn_i = qi * 4 + fog
                            if dn_i >= 2:
                                pq, pfog = divmod(dn_i - 2, 4)
                                ctx.wait(e, "td", E[("dnmm", pq, pfog, 3, 1)])
                            ctx.inc(e.dma_start(
                                out=fdb[dn_i % 2][:, 0:n_itq * 512].rearrange("p (i c) -> p i c", i=n_itq),
                                in_=d3[:, off:off + n_itq, 512 * fog:512 * (fog + 1)]), f"{tag}_fD{dn_i % 2}", 16)
                        else:
                            qi, fog, fs, th = it_[1], it_[2], it_[3], it_[4]
                            un = dn_unit_index(qi, fog, fs, th)
                            ctx.wait(e, "vd", E[("dnev", qi, fog, fs, th)])
                            if un >= 32:
                                e.wait_ge(sAccR[un % 4], 16 * (un // 4 - 7))
                            fo = 4 * fog + fs
                            dst = ffn_out_ap.rearrange("(f p) t -> p f t", p=128)[:, fo, 512 * th:512 * (th + 1)]
                            op = ALU.bypass if qi == 0 else ALU.add
                            dd = e.dma_start(out=dst, in_=fstg[un % 4][:, :], accum_op=op)
                            ctx.inc(dd, f"{tag}_fA{un % 4}", 16)
                    for i in range(4):
                        e.wait_ge(sAccR[i], 16 * (ctx.val(f"{tag}_fA{i}") // 16))
                e.wait_ge(sO[0], 16 * (NTC * NH1 // 2))
                e.wait_ge(sO[1], 16 * (NTC * NH1 // 2))
                ctx.fence(e, tag)
                cc = e.collective_compute("AllGather", ALU.bypass, replica_groups=RG,
                                          ins=[agin_ap], outs=[agout_ap])
                ctx.inc(cc, f"{tag}_C")
                e.wait_ge(sC, ctx.val(f"{tag}_C"))
                ctx.close_phase(e)

            @block.scalar
            def _(e):
                ctx.open_phase(e)
                act_walk(emit=True, e=e)
                assert ctx.val("ad") == E["ad_end"], (ctx.val("ad"), E["ad_end"])
                ctx.close_phase(e)

            @block.tensor
            def _(e):
                ctx.open_phase(e)
                pe_walk(emit=True, e=e)
                assert ctx.val("td") == E["td_end"], (ctx.val("td"), E["td_end"])
                ctx.close_phase(e)

            @block.vector
            def _(e):
                ctx.open_phase(e)
                dve_walk(emit=True, e=e)
                assert ctx.val("vd") == E["vd_end"], (ctx.val("vd"), E["vd_end"])
                ctx.close_phase(e)
    ctx.end_phase()

def ph_resout(ctx, tag, src_aps, y_ap):
    nc = ctx.nc
    nsrc = len(src_aps)
    with ExitStack() as st:
        bufs = [st.enter_context(nc.sbuf_tensor(f"{tag}_b{i}", [128, NKC * TL], F32)) for i in range(nsrc)]
        sL = ctx.new_sem(f"{tag}_L")
        sS = ctx.new_sem(f"{tag}_S")
        vd0 = ctx.val("vd")
        vd_f = [vd0 + f + 1 for f in range(NKC)]
        with nc.Block() as block:
            @block.sync
            def _(e):
                ctx.open_phase(e)
                for si, sap in enumerate(src_aps):
                    s3 = sap.rearrange("(f p) t -> p f t", p=128)
                    d3 = bufs[si][:, :].rearrange("p (f t) -> p f t", f=NKC)
                    for c4 in range(4):
                        ctx.inc(e.dma_start(out=d3[:, 4 * c4:4 * c4 + 4, :],
                                            in_=s3[:, 4 * c4:4 * c4 + 4, :]), f"{tag}_L", 16)
                y3 = y_ap.rearrange("(f p) t -> p f t", p=128)
                b3 = bufs[0][:, :].rearrange("p (f t) -> p f t", f=NKC)
                for f in range(NKC):
                    ctx.wait(e, "vd", vd_f[f])
                    ctx.inc(e.dma_start(out=y3[:, f:f + 1, :], in_=b3[:, f:f + 1, :]), f"{tag}_S", 16)
                e.wait_ge(sS, 16 * NKC)
                ctx.close_phase(e)

            @block.vector
            def _(e):
                ctx.open_phase(e)
                e.wait_ge(sL, 64 * nsrc)
                for f in range(NKC):
                    sl = slice(f * TL, (f + 1) * TL)
                    for si in range(1, nsrc):
                        tt = e.tensor_tensor(bufs[0][:, sl], bufs[0][:, sl], bufs[si][:, sl], ALU.add)
                    ctx.inc(tt, "vd")
                ctx.close_phase(e)

            for eng_name in ["scalar", "tensor", "gpsimd"]:
                getattr(block, eng_name)(lambda e: ctx.close_phase(e))
    ctx.end_phase()


def ph_copy(ctx, tag, dst_ap, src_ap):
    nc = ctx.nc
    sK = ctx.new_sem(f"{tag}_K")
    with nc.Block() as block:
        @block.gpsimd
        def _(e):
            ctx.open_phase(e)
            ctx.inc(e.dma_start(out=dst_ap, in_=src_ap), f"{tag}_K", 16)
            e.wait_ge(sK, 16)
            ctx.close_phase(e)

        for eng_name in ["sync", "vector", "scalar", "tensor"]:
            getattr(block, eng_name)(lambda e: ctx.close_phase(e))
    ctx.end_phase()


def _g0(ctx, t):
    ph_init(ctx)
    ph_ln(ctx, "ln0", [t["xt"]], t["xn0_loc_i"], t["xn0_ag"])
    ph_qkv0(ctx, "qkv0", t["xn0_ag"], t["w0"], t["cos4"], t["sinF4"],
            t["q0"], t["k0"], t["v0"])
    ph_copy(ctx, "cp0", t["xn0_loc"], t["xn0_loc_i"])


def _g1(ctx, t):
    ph_init(ctx)
    ph_attn0(ctx, "at0", t["q0"], t["k0"], t["v0"], t["mask4"], t["o0_agin"], t["o0_ag"])
    ph_gather_womm(ctx, "wo0", t["o0_ag"], t["gidx"], t["wo"], t["attn0_d"])


def _g2(ctx, t):
    ph_init(ctx)
    ph_ffn(ctx, "ff0", t["xn0_loc"],
           [(t["shgT"], t["shuT"], t["shdT"], INTER_SH // 128),
            (t["rgT"], t["ruT"], t["rdT"], INTER_R // 128)], t["ffn0_d"])


def _g3(ctx, t):
    ph_init(ctx)
    ph_ln(ctx, "ln1", [t["xt"], t["attn0_d"], t["ffn0_d"]], t["xn1_loc_i"], t["xn1_ag"],
          x1_ap=t["x1"])
    ph_qkv_scan1(ctx, "sc1", t["xn1_ag"], t["w1"], t["dec"], t["o1_agin"], t["o1_ag"])
    ph_gather_womm(ctx, "wo1", t["o1_ag"], t["gidx"], t["w_out"], t["attn1_d"])
    ph_copy(ctx, "cp1", t["xn1_loc"], t["xn1_loc_i"])


def _g4(ctx, t):
    ph_init(ctx)
    ph_ghost(ctx, "gh", {"g": t["gidxT_g"], "u": t["gidxT_u"], "d": t["gidxT_d"]},
             t["lut"], {"g": t["gw_g_in"], "u": t["gw_u_in"], "d": t["gw_d_in"]},
             {"g": t["gw_g"], "u": t["gw_u"], "d": t["gw_d"]})
    ph_ffn(ctx, "ff1", t["xn1_loc"],
           [(t["gw_g"], t["gw_u"], t["gw_d"], 43)], t["ffn1_d"])


def _g5(ctx, t):
    ph_init(ctx)
    ph_resout(ctx, "ro", [t["x1"], t["attn1_d"], t["ffn1_d"]], t["y"])


GROUPS = [
    ("g0", ["xt", "w0", "cos4", "sinF4"], ["xn0_loc", "q0", "k0", "v0"], _g0,
     ["xn0_loc_i", "xn0_ag"]),
    ("g1", ["q0", "k0", "v0", "mask4", "gidx", "wo"], ["attn0_d"], _g1,
     ["o0_agin", "o0_ag"]),
    ("g2", ["xn0_loc", "shgT", "shuT", "shdT", "rgT", "ruT", "rdT"], ["ffn0_d"], _g2, []),
    ("g3", ["xt", "attn0_d", "ffn0_d", "w1", "dec", "gidx", "w_out"],
     ["x1", "xn1_loc", "attn1_d"], _g3, ["xn1_loc_i", "xn1_ag", "o1_agin", "o1_ag"]),
    ("g4", ["xn1_loc", "gidxT_g", "gidxT_u", "gidxT_d", "lut"], ["ffn1_d"], _g4,
     ["gw_g_in", "gw_u_in", "gw_d_in", "gw_g", "gw_u", "gw_d"]),
    ("g5", ["x1", "attn1_d", "ffn1_d"], ["y"], _g5, []),
]

INTERNAL_SHAPES = {
    "xn0_loc_i": ([C, TL], BF16), "xn1_loc_i": ([C, TL], BF16),
}


def build_group(name, ext_in, ext_out, builder, internals):
    nc = bass.Bass("TRN2", target_bir_lowering=False)
    stack = ExitStack()
    t = {}
    for n in ext_in:
        t[n] = nc.dram_tensor(n, *TENSORS[n], kind="ExternalInput").ap()
    for n in ext_out:
        t[n] = nc.dram_tensor(n, *TENSORS[n], kind="ExternalOutput").ap()
    for n in internals:
        shape, dt = INTERNAL_SHAPES.get(n, TENSORS.get(n, (None, None)))
        t[n] = nc.dram_tensor(n, shape, dt, kind="Internal").ap()
    with stack:
        ctx = Ctx(nc, stack)
        register_consts(ctx)
        builder(ctx, t)
    return nc


_NC_CACHE = {}


def run_groups(data, upto=None, trace=False):
    for gi, (name, ext_in, ext_out, builder, internals) in enumerate(GROUPS):
        if upto is not None and gi > upto:
            break
        if name not in _NC_CACHE:
            _NC_CACHE[name] = build_group(name, ext_in, ext_out, builder, internals)
        nc = _NC_CACHE[name]
        in_maps = [{n: np.ascontiguousarray(data[j][n]) for n in ext_in} for j in range(NCORES)]
        res = run_bass_kernel_spmd(nc, in_maps, core_ids=list(range(NCORES)), trace=trace)
        for j in range(NCORES):
            for n in ext_out:
                data[j][n] = res.results[j][n]
    return data


def kernel(**inputs):
    shared, percore = prep_inputs(inputs)
    data = []
    for j in range(NCORES):
        d = dict(shared)
        d.update(percore[j])
        data.append(d)
    run_fused(data)
    out = np.concatenate([np.asarray(data[j]["y"]) for j in range(NCORES)], axis=1)
    return np.ascontiguousarray(out.T).reshape(1, T, C).astype(np.float32)


FUSED_EXT_IN = ["xt", "w0", "cos4", "sinF4", "mask4", "gidx", "wo",
                "shgT", "shuT", "shdT", "rgT", "ruT", "rdT",
                "w1", "dec", "w_out", "gidxT_g", "gidxT_u", "gidxT_d", "lut"]
FUSED_INTERNAL = ["gib_g", "gib_u", "gib_d", "xn0_loc_i", "xn0_ag", "q0", "k0", "v0", "o0_agin", "o0_ag",
                  "attn0_d", "ffn0_d", "x1", "xn1_loc_i", "xn1_ag",
                  "o1_agin", "o1_ag", "gw_g_in", "gw_u_in", "gw_d_in",
                  "gw_g", "gw_u", "gw_d", "attn1_d", "ffn1_d"]


def _fused(ctx, t):
    ph_init(ctx)
    ph_ln(ctx, "ln0", [t["xt"]], t["xn0_loc_i"], t["xn0_ag"])
    ph_qkv0(ctx, "qkv0", t["xn0_ag"], t["w0"], t["cos4"], t["sinF4"],
            t["q0"], t["k0"], t["v0"],
            idx_casts=[(t["gib_g"], t["gidxT_g"]), (t["gib_u"], t["gidxT_u"]),
                       (t["gib_d"], t["gidxT_d"])])
    ph_attn_ffn0(ctx, "af0", t["q0"], t["k0"], t["v0"], t["mask4"], t["o0_agin"], t["o0_ag"],
                 t["xn0_loc_i"],
                 [(t["shgT"], t["shuT"], t["shdT"], INTER_SH // 128),
                  (t["rgT"], t["ruT"], t["rdT"], INTER_R // 128)], t["ffn0_d"],
                 gidx_aps={"g": t["gib_g"], "u": t["gib_u"], "d": t["gib_d"]},
                 lut_ap=t["lut"],
                 gagin_aps={"g": t["gw_g_in"], "u": t["gw_u_in"], "d": t["gw_d_in"]},
                 gagout_aps={"g": t["gw_g"], "u": t["gw_u"], "d": t["gw_d"]})
    ph_gather_womm(ctx, "wo0", t["o0_ag"], t["gidx"], t["wo"], t["attn0_d"])
    ph_ln(ctx, "ln1", [t["xt"], t["attn0_d"], t["ffn0_d"]], t["xn1_loc_i"], t["xn1_ag"],
          x1_ap=t["x1"])
    ph_qkv_scan1(ctx, "sc1", t["xn1_ag"], t["w1"], t["dec"], t["o1_agin"], t["o1_ag"],
                 xnloc_ap=t["xn1_loc_i"],
                 fparts=[(t["gw_g"], t["gw_u"], t["gw_d"], INTER_G // 128)],
                 ffn_out_ap=t["ffn1_d"])
    ph_gather_womm(ctx, "wo1", t["o1_ag"], t["gidx"], t["w_out"], t["y"],
                   extra_src_aps=[t["x1"], t["ffn1_d"]])


def build_fused():
    nc = bass.Bass("TRN2", target_bir_lowering=False)
    t = {}
    for n in FUSED_EXT_IN:
        t[n] = nc.dram_tensor(n, *TENSORS[n], kind="ExternalInput").ap()
    t["y"] = nc.dram_tensor("y", *TENSORS["y"], kind="ExternalOutput").ap()
    for n in FUSED_INTERNAL:
        shape, dt = INTERNAL_SHAPES.get(n, TENSORS.get(n, (None, None)))
        t[n] = nc.dram_tensor(n, shape, dt, kind="Internal").ap()
    with ExitStack() as stack:
        ctx = Ctx(nc, stack)
        register_consts(ctx)
        _fused(ctx, t)
    return nc


def run_fused(data, trace=False):
    if "fused" not in _NC_CACHE:
        _NC_CACHE["fused"] = build_fused()
    nc = _NC_CACHE["fused"]
    in_maps = [{n: np.ascontiguousarray(data[j][n]) for n in FUSED_EXT_IN}
               for j in range(NCORES)]
    res = run_bass_kernel_spmd(nc, in_maps, core_ids=list(range(NCORES)), trace=trace)
    for j in range(NCORES):
        data[j]["y"] = res.results[j]["y"]
    return res


def ph_attn_ffn0(ctx, tag, q_ap, k_ap, v_ap, mask_ap, agin_ap, agout_ap,
                 xnloc_ap, parts, ffn_out_ap,
                 gidx_aps=None, lut_ap=None, gagin_aps=None, gagout_aps=None):
    nc = ctx.nc
    aux = ctx.aux
    NQC = 16
    pairs = [(qc, h) for qc in range(NQC) for h in range(NQH)]

    quarters = []
    for pi, (_, _, _, n_it) in enumerate(parts):
        for off in range(0, n_it, 8):
            quarters.append((pi, off, min(8, n_it - off)))
    ffn_units = []
    for qi, (pi, off, n_itq) in enumerate(quarters):
        for itl in range(n_itq):
            for th in range(2):
                ffn_units.append(("gu", qi, itl, th))
        for fog in range(4):
            for fs in range(4):
                for th in range(2):
                    ffn_units.append(("dn", qi, fog, fs, th))
    n_batches = sum(2 * (qc + 1) for qc, h in pairs)
    gtiles = []
    if gidx_aps is not None:
        for key, nrows, ncols in [("g", 256, INTER_G), ("u", 256, INTER_G), ("d", 688, C)]:
            r = 0
            while r < nrows:
                nr = min(128, nrows - r)
                if ncols > 2752:
                    gtiles.append((key, r, nr, 0, 2752))
                    gtiles.append((key, r, nr, 2752, ncols - 2752))
                else:
                    gtiles.append((key, r, nr, 0, ncols))
                r += 128
    ghost_ops = [(t, b) for t in range(len(gtiles)) for b in range(16)]

    with ExitStack() as st:
        qf = [st.enter_context(nc.sbuf_tensor(f"{tag}_qf{i}", [128, T], BF16)) for i in range(2)]
        kf = st.enter_context(nc.sbuf_tensor(f"{tag}_kf", [128, T], BF16))
        v_sb = st.enter_context(nc.sbuf_tensor(f"{tag}_v", [128, 64 * 65], BF16))
        m_sb = st.enter_context(nc.sbuf_tensor(f"{tag}_m", [128, 4 * 512], BF16))
        p_sb = [st.enter_context(nc.sbuf_tensor(f"{tag}_p{i}", [128, 1024], BF16)) for i in range(2)]
        otmp = [st.enter_context(nc.sbuf_tensor(f"{tag}_ot{i}", [65, 512], F32)) for i in range(2)]
        rrow = [st.enter_context(nc.sbuf_tensor(f"{tag}_rr{i}", [1, 512], F32)) for i in range(2)]
        finb = [st.enter_context(nc.sbuf_tensor(f"{tag}_fin{i}", [64, 512], BF16)) for i in range(2)]
        xn_sb = st.enter_context(nc.sbuf_tensor(f"{tag}_xn", [128, NKC * TL], BF16))
        h_sb = st.enter_context(nc.sbuf_tensor(f"{tag}_h", [128, 8 * TL], BF16))
        gb = [st.enter_context(nc.sbuf_tensor(f"{tag}_g{i}", [128, NKC * 128], BF16)) for i in range(2)]
        ub = [st.enter_context(nc.sbuf_tensor(f"{tag}_u{i}", [128, NKC * 128], BF16)) for i in range(2)]
        db = [st.enter_context(nc.sbuf_tensor(f"{tag}_d{i}", [128, 8 * 512], BF16)) for i in range(2)]
        sg = st.enter_context(nc.sbuf_tensor(f"{tag}_sg", [128, 512], F32))
        stg = [st.enter_context(nc.sbuf_tensor(f"{tag}_st{i}", [128, 512], F32)) for i in range(8)]
        if gidx_aps is not None:
            gh_idx = [st.enter_context(nc.sbuf_tensor(f"{tag}_ghi{i}", [128, 2752], BF16)) for i in range(2)]
            gh_w = [st.enter_context(nc.sbuf_tensor(f"{tag}_ghw{i}", [128, 2752], BF16)) for i in range(2)]
            gh_eq = st.enter_context(nc.sbuf_tensor(f"{tag}_ghe", [128, 2752], BF16))
            gh_lrep = st.enter_context(nc.sbuf_tensor(f"{tag}_ghl", [128, 16], F32))
            gh_lrow = st.enter_context(nc.sbuf_tensor(f"{tag}_ghr", [1, 16], F32))
            sGI = ctx.new_sem(f"{tag}_GI")
            sGSp = [ctx.new_sem(f"{tag}_GS{i}") for i in range(2)]
        S_ps = [st.enter_context(nc.psum_tensor(f"{tag}_S{i}", [128, 1024], F32)) for i in range(2)]
        O_ps = [st.enter_context(nc.psum_tensor(f"{tag}_O{i}", [65, 512], F32)) for i in range(2)]
        f_ps = [st.enter_context(nc.psum_tensor(f"{tag}_f{i}", [128, 512], F32)) for i in range(2)]
        sLA = ctx.new_sem(f"{tag}_L")
        sP = ctx.new_sem(f"{tag}_P")
        sPf = [ctx.new_sem(f"{tag}_Pf{i}") for i in range(2)]
        sC = ctx.new_sem(f"{tag}_C")
        sX = ctx.new_sem(f"{tag}_X")
        sGU = [ctx.new_sem(f"{tag}_GU{i}") for i in range(2)]
        sD = [ctx.new_sem(f"{tag}_D{i}") for i in range(2)]
        sAccR = [ctx.new_sem(f"{tag}_A{i}") for i in range(8)]

        E = {}
        posl = {}

        def pe_walk(emit, e=None):
            td = ctx.val("td")
            gb_i = [0]
            fu = [0]
            gS = 0
            per_slot = (len(ffn_units) + n_batches - 1) / n_batches

            def mm_inc(key):
                nonlocal td
                td += 1
                if not emit:
                    E[key] = td

            def wait(sem_key, v):
                if emit and v > 0:
                    ctx.wait(e, sem_key, v)

            def wait_h(sem, v):
                if emit:
                    e.wait_ge(sem, v)

            def emit_ffn_unit(slot):
                u = ffn_units[fu[0]]
                if u[0] == "gu":
                    _, qi, itl, th = u
                    pi, off, n_itq = quarters[qi]
                    g_i = gb_i[0]
                    if th == 0:
                        wait_h(sGU[g_i % 2], 32 * (g_i // 2 + 1))
                    key_prev = ("gu", qi, itl, th)
                    wait("ad", E.get(("silu_prev_for",) + key_prev, 0) if not emit else E.get(("silu_prev_for",) + key_prev, 0))
                    prev = None
                    if (qi, itl, th) != (0, 0, 0):
                        if th == 1:
                            prev = (qi, itl, 0)
                        elif itl > 0:
                            prev = (qi, itl - 1, 1)
                        else:
                            prev = (qi - 1, quarters[qi - 1][2] - 1, 1)
                    if prev is not None:
                        wait("ad", E.get(("silu",) + prev, 0))
                        wait("vd", E.get(("hmul",) + prev, 0))
                    elif gidx_aps is not None:
                        wait("vd", E.get("glrep", 0))
                    if itl == 0 and th == 0 and qi > 0:
                        wait("vd", E.get(("dnev", qi - 1, 3, 3, 1), 0))
                    for kc in range(NKC):
                        if emit:
                            e.matmul(f_ps[0][:, :], gb[g_i % 2][:, kc * 128:(kc + 1) * 128],
                                     xn_sb[:, kc * TL + 512 * th:kc * TL + 512 * (th + 1)],
                                     start=(kc == 0), stop=(kc == NKC - 1))
                    for kc in range(NKC):
                        if emit:
                            mm = e.matmul(f_ps[1][:, :], ub[g_i % 2][:, kc * 128:(kc + 1) * 128],
                                          xn_sb[:, kc * TL + 512 * th:kc * TL + 512 * (th + 1)],
                                          start=(kc == 0), stop=(kc == NKC - 1))
                    if emit:
                        ctx.cnt["td"] += 1
                        mm.then_inc(ctx.sem("td"), 1)
                    else:
                        mm_inc(("gumm", qi, itl, th))
                        posl[("gumm", qi, itl, th)] = slot
                    if th == 1:
                        gb_i[0] += 1
                else:
                    _, qi, fog, fs, th = u
                    pi, off, n_itq = quarters[qi]
                    dn_i = qi * 4 + fog
                    if fs == 0 and th == 0:
                        wait_h(sD[dn_i % 2], 16 * (dn_i // 2 + 1))
                        if fog == 0:
                            lastgu = (qi, n_itq - 1, 1)
                            wait("ad", E.get(("silu",) + lastgu, 0))
                            wait("vd", E.get(("hmul",) + lastgu, 0))
                    un = dn_unit_index(qi, fog, fs, th)
                    if un >= 2:
                        wait("vd", E.get(("dnev_u", un - 2), 0))
                    for itl in range(n_itq):
                        if emit:
                            mm = e.matmul(f_ps[un % 2][:, :],
                                          db[dn_i % 2][:, itl * 512 + 128 * fs:itl * 512 + 128 * (fs + 1)],
                                          h_sb[:, itl * TL + 512 * th:itl * TL + 512 * (th + 1)],
                                          start=(itl == 0), stop=(itl == n_itq - 1))
                    if emit:
                        ctx.cnt["td"] += 1
                        mm.then_inc(ctx.sem("td"), 1)
                    else:
                        mm_inc(("dnmm", qi, fog, fs, th))
                        posl[("dnmm", qi, fog, fs, th)] = slot
                fu[0] += 1

            if emit:
                wait_h(sLA, 80 + (16 if gidx_aps is not None else 0))
                wait_h(sX, 64)
            if gidx_aps is not None:
                if emit:
                    mm = e.matmul(f_ps[0][:, 0:16], aux["onesr"][:, :], gh_lrow[:, :], start=True, stop=True)
                    ctx.cnt["td"] += 1
                    mm.then_inc(ctx.sem("td"), 1)
                else:
                    mm_inc("glut")
            for idx, (qc, h) in enumerate(pairs):
                B = 2 * (qc + 1)
                qslice = qf[h // 2][64 * (h % 2):64 * (h % 2) + 64, 512 * qc:512 * (qc + 1)]
                order = [("S", 0), ("S", 1)]
                for b in range(B - 2):
                    order.append(("PV", b))
                    order.append(("S", b + 2))
                order.append(("PV", B - 2))
                order.append(("PV", B - 1))
                for kind, b in order:
                    g = gS + b
                    if kind == "S":
                        slot = g
                        while fu[0] < len(ffn_units) and fu[0] < per_slot * (slot + 1):
                            emit_ffn_unit(slot)
                        if g >= 2:
                            wait("ad", E.get(("exp", g - 2), 0))
                        for i in range(2):
                            kt = 2 * b + i
                            if emit:
                                mm = e.matmul(S_ps[g % 2][:, 512 * i:512 * (i + 1)],
                                              kf[64 * (h % 2):64 * (h % 2) + 64, 128 * kt:128 * (kt + 1)],
                                              qslice, start=True, stop=True)
                        if emit:
                            ctx.cnt["td"] += 1
                            mm.then_inc(ctx.sem("td"), 1)
                        else:
                            mm_inc(("S", g))
                    else:
                        wait("ad", E.get(("exp", g), 0))
                        for i in range(2):
                            kt = 2 * b + i
                            rel = kt - 4 * qc
                            if rel >= 0:
                                wait("vd", E.get(("mask", idx, rel), 0))
                            if kt == 0 and idx >= 2:
                                wait("vd", E.get(("fin", idx - 2), 0))
                            if emit:
                                mm = e.matmul(O_ps[idx % 2][:, :],
                                              v_sb[:, 65 * kt:65 * (kt + 1)],
                                              p_sb[g % 2][:, 512 * i:512 * (i + 1)],
                                              start=(kt == 0), stop=(kt == 4 * (qc + 1) - 1))
                        if emit:
                            ctx.cnt["td"] += 1
                            mm.then_inc(ctx.sem("td"), 1)
                        else:
                            mm_inc(("PV", idx, b))
                wait("ad", E.get(("recip", idx), 0))
                if idx >= 2:
                    wait("vd", E.get(("fin", idx - 2), 0))
                if emit:
                    mm = e.matmul(O_ps[idx % 2][0:64, :], aux["onesr"][0:1, 0:64],
                                  rrow[idx % 2][0:1, :], start=True, stop=True)
                    ctx.cnt["td"] += 1
                    mm.then_inc(ctx.sem("td"), 1)
                else:
                    mm_inc(("bc", idx))
                    posl[("pair", idx)] = gS + B - 1
                gS += B
            while fu[0] < len(ffn_units):
                emit_ffn_unit(n_batches - 1)
            if not emit:
                E["td_end"] = td

        def dn_unit_index(qi, fog, fs, th):
            return (qi * 4 + fog) * 8 + fs * 2 + th

        pe_walk(emit=False)

        batch_of_pair = {}
        gS = 0
        for idx, (qc, h) in enumerate(pairs):
            batch_of_pair[idx] = (gS, 2 * (qc + 1))
            gS += 2 * (qc + 1)

        def act_walk(emit, e=None):
            ad = ctx.val("ad")
            fu = 0
            silu_seq = [u for u in ffn_units if u[0] == "gu"]

            def inc(key):
                nonlocal ad
                ad += 1
                if not emit:
                    E[key] = ad

            def wait(sem_key, v):
                if emit and v > 0:
                    ctx.wait(e, sem_key, v)

            for idx, (qc, h) in enumerate(pairs):
                g0, B = batch_of_pair[idx]
                for b in range(B):
                    g = g0 + b
                    while fu < len(silu_seq):
                        _, qi, itl, th = silu_seq[fu]
                        if posl.get(("gumm", qi, itl, th), 10 ** 9) <= g:
                            wait("td", E[("gumm", qi, itl, th)])
                            prev = None
                            if (qi, itl, th) != (0, 0, 0):
                                if th == 1:
                                    prev = (qi, itl, 0)
                                elif itl > 0:
                                    prev = (qi, itl - 1, 1)
                                else:
                                    prev = (qi - 1, quarters[qi - 1][2] - 1, 1)
                            if prev is not None:
                                wait("vd", E.get(("hmul",) + prev, 0))
                            if emit:
                                a = e.activation(sg[:, :], f_ps[0][:, :], AF.Silu)
                                ctx.cnt["ad"] += 1
                                a.then_inc(ctx.sem("ad"), 1)
                            else:
                                inc(("silu", qi, itl, th))
                            fu += 1
                        else:
                            break
                    wait("td", E[("S", g)])
                    if g >= 2:
                        wait("td", E.get(("PVflat", g - 2), 0))
                    if emit:
                        a = e.activation(p_sb[g % 2][:, :], S_ps[g % 2][:, :], AF.Exp,
                                         bias=EXP_BIAS, scale=EXP_SCALE)
                        ctx.cnt["ad"] += 1
                        a.then_inc(ctx.sem("ad"), 1)
                    else:
                        inc(("exp", g))
                wait("td", E[("PV", idx, B - 1)])
                if idx >= 2:
                    wait("vd", E.get(("fin", idx - 2), 0))
                if emit:
                    a = e.copy(otmp[idx % 2][:, :], O_ps[idx % 2][:, :])
                    ctx.cnt["ad"] += 1
                    a.then_inc(ctx.sem("ad"), 1)
                    e.activation(rrow[idx % 2][0:1, :], otmp[idx % 2][64:65, :], AF.Ln)
                    a = e.activation(rrow[idx % 2][0:1, :], rrow[idx % 2][0:1, :], AF.Exp, scale=-1.0)
                    ctx.cnt["ad"] += 2
                    a.then_inc(ctx.sem("ad"), 2)
                else:
                    inc(("evac", idx))
                    ad += 2
                    E[("recip", idx)] = ad
            while fu < len(silu_seq):
                _, qi, itl, th = silu_seq[fu]
                wait("td", E[("gumm", qi, itl, th)])
                prev = None
                if (qi, itl, th) != (0, 0, 0):
                    if th == 1:
                        prev = (qi, itl, 0)
                    elif itl > 0:
                        prev = (qi, itl - 1, 1)
                    else:
                        prev = (qi - 1, quarters[qi - 1][2] - 1, 1)
                if prev is not None:
                    wait("vd", E.get(("hmul",) + prev, 0))
                if emit:
                    a = e.activation(sg[:, :], f_ps[0][:, :], AF.Silu)
                    ctx.cnt["ad"] += 1
                    a.then_inc(ctx.sem("ad"), 1)
                else:
                    inc(("silu", qi, itl, th))
                fu += 1
            if not emit:
                E["ad_end"] = ad

        def build_pv_flat():
            gS = 0
            for idx, (qc, h) in enumerate(pairs):
                B = 2 * (qc + 1)
                for b in range(B):
                    E[("PVflat", gS + b)] = E[("PV", idx, b)]
                gS += B

        def dve_walk(emit, e=None):
            vd = ctx.val("vd")
            hm = 0
            dnc = 0
            gho = 0

            def inc(key):
                nonlocal vd
                vd += 1
                if not emit:
                    E[key] = vd

            def wait(sem_key, v):
                if emit and v > 0:
                    ctx.wait(e, sem_key, v)

            def ghost_weave(idx):
                nonlocal gho, dnc
                if gidx_aps is None:
                    return
                due = (len(ghost_ops) * (idx + 1) + 45) // 46 if idx < 10 ** 8 else len(ghost_ops)
                while gho < min(due, len(ghost_ops)):
                    t, b_ = ghost_ops[gho]
                    key, r0, nr, c0, nc_ = gtiles[t]
                    if b_ == 0 and t >= 2 and dnc <= need_dn[t]:
                        break
                    if emit:
                        if b_ == 0:
                            e.wait_ge(sGI, 16 * (t + 1))
                            if t >= 2:
                                e.wait_ge(sGSp[t % 2], 16 * (t // 2))
                        e.tensor_scalar(gh_eq[0:nr, 0:nc_], gh_idx[t % 2][0:nr, 0:nc_],
                                        float(b_), gh_lrep[0:nr, b_:b_ + 1], ALU.is_equal, ALU.mult)
                        if b_ == 0:
                            tt = e.tensor_scalar(gh_w[t % 2][0:nr, 0:nc_], gh_eq[0:nr, 0:nc_], 1.0, None, ALU.mult)
                        else:
                            tt = e.tensor_tensor(gh_w[t % 2][0:nr, 0:nc_], gh_w[t % 2][0:nr, 0:nc_],
                                                 gh_eq[0:nr, 0:nc_], ALU.add)
                        if b_ == 15:
                            ctx.cnt["vd"] += 1
                            tt.then_inc(ctx.sem("vd"), 1)
                    else:
                        if b_ == 15:
                            inc(("gtile", t))
                    gho += 1

            def weave(g):
                nonlocal hm, dnc
                while hm < len(ffn_units):
                    u = ffn_units[hm]
                    if u[0] == "gu":
                        _, qi, itl, th = u
                        if posl.get(("gumm", qi, itl, th), 10 ** 9) > g:
                            break
                        wait("ad", E[("silu", qi, itl, th)])
                        if emit:
                            tt = e.tensor_tensor(h_sb[:, itl * TL + 512 * th:itl * TL + 512 * (th + 1)],
                                                 sg[:, :], f_ps[1][:, :], ALU.mult)
                            ctx.cnt["vd"] += 1
                            tt.then_inc(ctx.sem("vd"), 1)
                        else:
                            inc(("hmul", qi, itl, th))
                    else:
                        _, qi, fog, fs, th = u
                        if posl.get(("dnmm", qi, fog, fs, th), 10 ** 9) > g:
                            break
                        un = dn_unit_index(qi, fog, fs, th)
                        wait("td", E[("dnmm", qi, fog, fs, th)])
                        if emit:
                            if un >= 8:
                                e.wait_ge(sAccR[un % 8], 16 * (un // 8))
                            tt = e.tensor_scalar(stg[un % 8][:, :], f_ps[un % 2][:, :], 1.0, None, ALU.mult)
                            ctx.cnt["vd"] += 1
                            tt.then_inc(ctx.sem("vd"), 1)
                        else:
                            inc(("dnev_u", un))
                            E[("dnev", qi, fog, fs, th)] = vd
                        dnc += 1
                    hm += 1

            if gidx_aps is not None:
                wait("td", E["glut"])
                if emit:
                    tt = e.tensor_scalar(gh_lrep[:, :], f_ps[0][:, 0:16], 1.0, None, ALU.mult)
                    ctx.cnt["vd"] += 1
                    tt.then_inc(ctx.sem("vd"), 1)
                else:
                    inc("glrep")
            for idx, (qc, h) in enumerate(pairs):
                g0, B = batch_of_pair[idx]
                for b in range(B):
                    weave(g0 + b)
                for rel in range(4):
                    kt = 4 * qc + rel
                    b = kt // 2
                    g = g0 + b
                    wait("ad", E[("exp", g)])
                    if emit:
                        i = kt % 2
                        tt = e.tensor_tensor(p_sb[g % 2][:, 512 * i:512 * (i + 1)],
                                             p_sb[g % 2][:, 512 * i:512 * (i + 1)],
                                             m_sb[:, 512 * rel:512 * (rel + 1)], ALU.mult)
                        ctx.cnt["vd"] += 1
                        tt.then_inc(ctx.sem("vd"), 1)
                    else:
                        inc(("mask", idx, rel))
                wait("td", E[("bc", idx)])
                wait("ad", E[("evac", idx)])
                if emit:
                    if idx >= 2:
                        e.wait_ge(sPf[idx % 2], 16 * (idx // 2))
                    tt = e.tensor_tensor(finb[idx % 2][:, :], otmp[idx % 2][0:64, :],
                                         O_ps[idx % 2][0:64, :], ALU.mult)
                    ctx.cnt["vd"] += 1
                    tt.then_inc(ctx.sem("vd"), 1)
                else:
                    inc(("fin", idx))
                ghost_weave(idx)
            weave(10 ** 9)
            ghost_weave(10 ** 9)
            if not emit:
                E["vd_end"] = vd

        ganchor_q = {}
        need_dn = {}
        if gidx_aps is not None:
            for t_ in range(len(gtiles)):
                p_start = 0
                while (len(ghost_ops) * (p_start + 1) + 45) // 46 <= 16 * t_:
                    p_start += 1
                slot = batch_of_pair[max(0, p_start - 1)][0]
                qa = len(quarters) - 1
                for qi_ in range(len(quarters)):
                    if posl.get(("dnmm", qi_, 3, 3, 1), 10 ** 9) >= slot:
                        qa = qi_
                        break
                ganchor_q[t_] = qa
                need_dn[t_] = (qa * 4 + 3) * 8 + 3 * 2 + 1
        act_walk(emit=False)
        build_pv_flat()
        dve_walk(emit=False)

        with nc.Block() as block:
            @block.sync
            def _(e):
                ctx.open_phase(e)
                for i in range(2):
                    ctx.inc(e.dma_start(out=qf[i][:, :], in_=q_ap[:, 128 * i:128 * (i + 1)],
                                        transpose=True), f"{tag}_L", 16)
                ctx.inc(e.dma_start(out=kf[:, :], in_=k_ap, transpose=True), f"{tag}_L", 16)
                ctx.inc(e.dma_start(out=v_sb[:, :].rearrange("p (s c) -> p s c", s=64),
                                    in_=v_ap.rearrange("(s p) c -> p s c", p=128)), f"{tag}_L", 16)
                ctx.inc(e.dma_start(out=m_sb[:, :].rearrange("p (r q) -> p r q", r=4),
                                    in_=mask_ap.rearrange("(r p) q -> p r q", p=128)), f"{tag}_L", 16)
                if gidx_aps is not None:
                    ctx.inc(e.dma_start(out=gh_lrow[:, :], in_=lut_ap), f"{tag}_L", 16)
                    anchors = {}
                    for t_ in range(len(gtiles)):
                        p_start = 0
                        while (len(ghost_ops) * (p_start + 1) + 45) // 46 <= 16 * t_:
                            p_start += 1
                        anchors[t_] = max(0, p_start - 2)
                    for t_ in (0, 1):
                        if t_ < len(gtiles):
                            key, r0, nr, c0, nc_ = gtiles[t_]
                            ctx.inc(e.dma_start(out=gh_idx[t_][0:nr, 0:nc_],
                                                in_=gidx_aps[key][r0:r0 + nr, c0:c0 + nc_]), f"{tag}_GI", 16)
                gld = 2
                gst = 0
                for idx, (qc, h) in enumerate(pairs):
                    ctx.wait(e, "vd", E[("fin", idx)])
                    dd = e.dma_start(out=agin_ap[64 * h:64 * (h + 1), 512 * qc:512 * (qc + 1)],
                                     in_=finb[idx % 2][:, :])
                    ctx.cnt[f"{tag}_Pf{idx % 2}"] += 16
                    dd.then_inc(sPf[idx % 2], 16)
                    while gidx_aps is not None and gld < len(gtiles) and anchors[gld] <= idx:
                        key, r0, nr, c0, nc_ = gtiles[gld]
                        ctx.wait(e, "vd", E[("gtile", gld - 2)])
                        ctx.inc(e.dma_start(out=gh_idx[gld % 2][0:nr, 0:nc_],
                                            in_=gidx_aps[key][r0:r0 + nr, c0:c0 + nc_]), f"{tag}_GI", 16)
                        gld += 1
                    while gidx_aps is not None and gst < len(gtiles) and \
                            E.get(("gtile", gst), 10 ** 9) <= E[("fin", idx)]:
                        key, r0, nr, c0, nc_ = gtiles[gst]
                        ctx.wait(e, "vd", E[("gtile", gst)])
                        ctx.inc(e.dma_start(out=gagin_aps[key][r0:r0 + nr, c0:c0 + nc_],
                                            in_=gh_w[gst % 2][0:nr, 0:nc_]), f"{tag}_GS{gst % 2}", 16)
                        gst += 1
                while gidx_aps is not None and gst < len(gtiles):
                    key, r0, nr, c0, nc_ = gtiles[gst]
                    ctx.wait(e, "vd", E[("gtile", gst)])
                    ctx.inc(e.dma_start(out=gagin_aps[key][r0:r0 + nr, c0:c0 + nc_],
                                        in_=gh_w[gst % 2][0:nr, 0:nc_]), f"{tag}_GS{gst % 2}", 16)
                    gst += 1
                e.wait_ge(sPf[0], 16 * 32)
                e.wait_ge(sPf[1], 16 * 32)
                if gidx_aps is not None:
                    e.wait_ge(sGSp[0], 16 * ((len(gtiles) + 1) // 2))
                    e.wait_ge(sGSp[1], 16 * (len(gtiles) // 2))
                ctx.close_phase(e)

            @block.gpsimd
            def _(e):
                ctx.open_phase(e)
                xn3 = xnloc_ap.rearrange("(f p) t -> p f t", p=128)
                xd3 = xn_sb[:, :].rearrange("p (f t) -> p f t", f=NKC)
                for c4 in range(4):
                    ctx.inc(e.dma_start(out=xd3[:, 4 * c4:4 * c4 + 4, :],
                                        in_=xn3[:, 4 * c4:4 * c4 + 4, :]), f"{tag}_X", 16)
                gu_seq = []
                for qi, (pi, off, n_itq) in enumerate(quarters):
                    for itl in range(n_itq):
                        gu_seq.append((qi, itl))
                items = []
                for k, (qi, itl) in enumerate(gu_seq):
                    items.append((posl[("gumm", qi, itl, 0)], 0, ("guload", k)))
                for qi, (pi, off, n_itq) in enumerate(quarters):
                    for fog in range(4):
                        items.append((posl[("dnmm", qi, fog, 0, 0)], 1, ("dnload", qi, fog)))
                        for fs in range(4):
                            for th in range(2):
                                items.append((posl[("dnmm", qi, fog, fs, th)] + 1, 2,
                                              ("accum", qi, fog, fs, th)))
                if gidx_aps is not None:
                    items.append((batch_of_pair[40][0], 3, ("gfence", 1)))
                    items.append((batch_of_pair[54][0], 3, ("gfence", 2)))
                items.sort(key=lambda x: (x[0], x[1]))
                for _, _, it_ in items:
                    if it_[0] == "gfence":
                        if it_[1] == 1:
                            e.wait_ge(sGSp[0], 16 * 4)
                            e.wait_ge(sGSp[1], 16 * 4)
                            ctx.fence(e, f"{tag}_gf1")
                            for key, nrows, ncols in [("g", 256, INTER_G), ("u", 256, INTER_G)]:
                                cc = e.collective_compute("AllGather", ALU.bypass, replica_groups=RG,
                                                          ins=[gagin_aps[key]], outs=[gagout_aps[key]])
                                ctx.inc(cc, f"{tag}_C")
                        else:
                            e.wait_ge(sGSp[0], 16 * ((len(gtiles) + 1) // 2))
                            e.wait_ge(sGSp[1], 16 * (len(gtiles) // 2))
                            ctx.fence(e, f"{tag}_gf2")
                            cc = e.collective_compute("AllGather", ALU.bypass, replica_groups=RG,
                                                      ins=[gagin_aps["d"]], outs=[gagout_aps["d"]])
                            ctx.inc(cc, f"{tag}_C")
                    elif it_[0] == "guload":
                        k = it_[1]
                        qi, itl = gu_seq[k]
                        pi, off, n_itq = quarters[qi]
                        gT_ap, uT_ap, dT_ap, n_it = parts[pi]
                        g3 = gT_ap.rearrange("(kc p) n -> p kc n", p=128)
                        u3 = uT_ap.rearrange("(kc p) n -> p kc n", p=128)
                        it = off + itl
                        if k >= 2:
                            pq, pitl = gu_seq[k - 2]
                            ctx.wait(e, "td", E[("gumm", pq, pitl, 1)])
                        ctx.inc(e.dma_start(out=gb[k % 2][:, :].rearrange("p (kc c) -> p kc c", kc=NKC),
                                            in_=g3[:, :, 128 * it:128 * (it + 1)]), f"{tag}_GU{k % 2}", 16)
                        ctx.inc(e.dma_start(out=ub[k % 2][:, :].rearrange("p (kc c) -> p kc c", kc=NKC),
                                            in_=u3[:, :, 128 * it:128 * (it + 1)]), f"{tag}_GU{k % 2}", 16)
                    elif it_[0] == "dnload":
                        qi, fog = it_[1], it_[2]
                        pi, off, n_itq = quarters[qi]
                        gT_ap, uT_ap, dT_ap, n_it = parts[pi]
                        d3 = dT_ap.rearrange("(i p) c -> p i c", p=128)
                        dn_i = qi * 4 + fog
                        if dn_i >= 2:
                            pq, pfog = divmod(dn_i - 2, 4)
                            ctx.wait(e, "td", E[("dnmm", pq, pfog, 3, 1)])
                        ctx.inc(e.dma_start(
                            out=db[dn_i % 2][:, 0:n_itq * 512].rearrange("p (i c) -> p i c", i=n_itq),
                            in_=d3[:, off:off + n_itq, 512 * fog:512 * (fog + 1)]), f"{tag}_D{dn_i % 2}", 16)
                    else:
                        qi, fog, fs, th = it_[1], it_[2], it_[3], it_[4]
                        un = dn_unit_index(qi, fog, fs, th)
                        ctx.wait(e, "vd", E[("dnev", qi, fog, fs, th)])
                        if un >= 32:
                            e.wait_ge(sAccR[un % 8], 16 * (un // 8 - 3))
                        fo = 4 * fog + fs
                        dst = ffn_out_ap.rearrange("(f p) t -> p f t", p=128)[:, fo, 512 * th:512 * (th + 1)]
                        op = ALU.bypass if qi == 0 else ALU.add
                        dd = e.dma_start(out=dst, in_=stg[un % 8][:, :], accum_op=op)
                        ctx.inc(dd, f"{tag}_A{un % 8}", 16)
                e.wait_ge(sPf[0], 16 * 32)
                e.wait_ge(sPf[1], 16 * 32)
                for i in range(8):
                    e.wait_ge(sAccR[i], 16 * (ctx.val(f"{tag}_A{i}") // 16))
                if gidx_aps is not None:
                    e.wait_ge(sGSp[0], 16 * ((len(gtiles) + 1) // 2))
                    e.wait_ge(sGSp[1], 16 * (len(gtiles) // 2))
                ctx.fence(e, tag)
                cc = e.collective_compute("AllGather", ALU.bypass, replica_groups=RG,
                                          ins=[agin_ap], outs=[agout_ap])
                ctx.inc(cc, f"{tag}_C")
                e.wait_ge(sC, ctx.val(f"{tag}_C"))
                ctx.close_phase(e)

            @block.tensor
            def _(e):
                ctx.open_phase(e)
                td0 = ctx.val("td")
                pe_walk(emit=True, e=e)
                assert ctx.val("td") == E["td_end"], (ctx.val("td"), E["td_end"])
                ctx.close_phase(e)

            @block.scalar
            def _(e):
                ctx.open_phase(e)
                act_walk(emit=True, e=e)
                assert ctx.val("ad") == E["ad_end"], (ctx.val("ad"), E["ad_end"])
                ctx.close_phase(e)

            @block.vector
            def _(e):
                ctx.open_phase(e)
                dve_walk(emit=True, e=e)
                assert ctx.val("vd") == E["vd_end"], (ctx.val("vd"), E["vd_end"])
                ctx.close_phase(e)
    ctx.end_phase()
